# version 04
# baseline (speedup 1.0000x reference)
import math
from contextlib import ExitStack

import numpy as np
import ml_dtypes

import concourse.bass as bass
import concourse.mybir as mybir
from concourse.bass_utils import run_bass_kernel_spmd

F32 = mybir.dt.float32
BF16 = mybir.dt.bfloat16
I32 = mybir.dt.int32
AF = mybir.ActivationFunctionType
OP = mybir.AluOpType
AX = mybir.AxisListType
BF = ml_dtypes.bfloat16

D = 4096
II = 14336
B = 4
S = 2048
NC = 8
SLOC = S // NC
T = B * SLOC
NTT = T // 128
NDT = D // 128
NST = 14
JLS = 8
TPASS = 512
NTTP = TPASS // 128
EPS = 1e-5
ATT_SCALE = 1.0 / math.sqrt(float(S))


class Ctr:
    def __init__(self, stack, nc, name):
        self.h = stack.enter_context(nc.semaphore(name))
        self.n = 0

    def inc(self, instr, k=1):
        instr.then_inc(self.h, k)
        self.n += k
        return self.n


def build_nc():
    nc = bass.Bass("TRN2", target_bir_lowering=False, debug=False)

    x_in = nc.dram_tensor("x_in", [T, D], F32, kind="ExternalInput").ap()
    gT = nc.dram_tensor("gT", [NST, 128, D], I32, kind="ExternalInput").ap()
    uT = nc.dram_tensor("uT", [NST, 128, D], I32, kind="ExternalInput").ap()
    dTt = nc.dram_tensor("dTt", [NDT, 128, NST, 128], I32, kind="ExternalInput").ap()
    gs_h = nc.dram_tensor("gs_h", [128, NST, NDT], F32, kind="ExternalInput").ap()
    gz_h = nc.dram_tensor("gz_h", [128, NST, NDT], I32, kind="ExternalInput").ap()
    us_h = nc.dram_tensor("us_h", [128, NST, NDT], F32, kind="ExternalInput").ap()
    uz_h = nc.dram_tensor("uz_h", [128, NST, NDT], I32, kind="ExternalInput").ap()
    ds_h = nc.dram_tensor("ds_h", [128, NDT, NST], F32, kind="ExternalInput").ap()
    dz_h = nc.dram_tensor("dz_h", [128, NDT, NST], I32, kind="ExternalInput").ap()
    idb = nc.dram_tensor("idb", [128, 128], BF16, kind="ExternalInput").ap()

    out = nc.dram_tensor("out", [T, D], F32, kind="ExternalOutput").ap()

    contrib = [nc.dram_tensor(f"ctb{s}", [1, 3, 128, D], BF16) for s in range(NST)]
    agout = [nc.dram_tensor(f"ago{s}", [NC, 3, 128, D], BF16) for s in range(NST)]
    ar_in = nc.dram_tensor("ar_in", [128, 512], F32)
    ar_out = nc.dram_tensor("ar_out", [128, 512], F32)
    park = nc.dram_tensor("park", [128, NTTP * D], F32)

    with ExitStack() as ctx:
        ent = ctx.enter_context

        psG = [ent(nc.psum_tensor(f"psG{i}", [128, 512], F32)) for i in range(2)]
        psU = [ent(nc.psum_tensor(f"psU{i}", [128, 512], F32)) for i in range(2)]
        psD = [ent(nc.psum_tensor(f"psD{i}", [128, 512], F32)) for i in range(2)]
        psT = [ent(nc.psum_tensor(f"psT{i}", [128, 1024], BF16)) for i in range(2)]

        sIN = Ctr(ctx, nc, "sIN")
        sIN2 = Ctr(ctx, nc, "sIN2")
        sGP = Ctr(ctx, nc, "sGP")
        sCTD = Ctr(ctx, nc, "sCTD")
        sCTG = Ctr(ctx, nc, "sCTG")
        sCTU = Ctr(ctx, nc, "sCTU")
        sAG = Ctr(ctx, nc, "sAG")
        sAR = Ctr(ctx, nc, "sAR")
        sT = Ctr(ctx, nc, "sT")
        sV = Ctr(ctx, nc, "sV")
        sSC = Ctr(ctx, nc, "sSC")
        sDN = [Ctr(ctx, nc, f"sDN{i}") for i in range(2)]
        sWG = [Ctr(ctx, nc, f"sWG{i}") for i in range(2)]
        sWU = [Ctr(ctx, nc, f"sWU{i}") for i in range(2)]
        sWD2 = [Ctr(ctx, nc, f"sWD{i}") for i in range(2)]
        sST = Ctr(ctx, nc, "sST")

        ident = ent(nc.sbuf_tensor("ident", [128, 128], BF16))
        nmean1 = ent(nc.sbuf_tensor("nmean1", [128, NTT], F32))
        rstd1 = ent(nc.sbuf_tensor("rstd1", [128, NTT], F32))
        nmean2 = ent(nc.sbuf_tensor("nmean2", [128, NTT], F32))
        rstd2 = ent(nc.sbuf_tensor("rstd2", [128, NTT], F32))
        ssum = ent(nc.sbuf_tensor("ssum", [128, NTT], F32))
        ssq = ent(nc.sbuf_tensor("ssq", [128, NTT], F32))
        svar = ent(nc.sbuf_tensor("svar", [128, NTT], F32))
        ssd = ent(nc.sbuf_tensor("ssd", [128, NTT], F32))
        S0 = ent(nc.sbuf_tensor("S0", [128, NDT, 16], F32))
        S1 = ent(nc.sbuf_tensor("S1", [128, NDT, 16], F32))
        Ssc = ent(nc.sbuf_tensor("Ssc", [128, NDT, 16], F32))
        NMX = ent(nc.sbuf_tensor("NMX", [128, NDT * 4], F32))
        SSM = ent(nc.sbuf_tensor("SSM", [128, NDT * 4], F32))
        R4 = ent(nc.sbuf_tensor("R4", [128, NDT * 4], F32))
        GRK = 8
        grt = ent(nc.sbuf_tensor("grt", [128, GRK * 128], F32))
        gr_incs = []
        trash = [ent(nc.sbuf_tensor(f"trash{i}", [128, D], BF16))
                 for i in range(2)]
        gsf = ent(nc.sbuf_tensor("gsf", [128, NST, NDT], F32))
        gzf = ent(nc.sbuf_tensor("gzf", [128, NST, NDT], F32))
        usf = ent(nc.sbuf_tensor("usf", [128, NST, NDT], F32))
        uzf = ent(nc.sbuf_tensor("uzf", [128, NST, NDT], F32))

        block = ent(nc.Block())

        with ExitStack() as p0:
            dsf = p0.enter_context(nc.sbuf_tensor("dsf", [128, NDT, NST], F32))
            dzf = p0.enter_context(nc.sbuf_tensor("dzf", [128, NDT, NST], F32))
            din = [p0.enter_context(nc.sbuf_tensor(f"din{i}", [128, NST * 128], I32))
                   for i in range(2)]
            ddq = [p0.enter_context(nc.sbuf_tensor(f"ddq{i}", [128, NST * 128], BF16))
                   for i in range(2)]
            stgd = p0.enter_context(nc.sbuf_tensor("stgd", [128, NST, D], BF16))

            @block.sync
            def _(sync):
                for dst, src in [(ident, idb), (gsf[:], gs_h), (usf[:], us_h),
                                 (dsf[:], ds_h)]:
                    sync.dma_start(out=dst if isinstance(dst, bass.AP) else dst[:],
                                   in_=src).then_inc(sIN.h, 16)
                    sIN.n += 16
            scales_in = sIN.n

            @block.gpsimd
            def _(gpsimd):
                for dst, src in [(gzf, gz_h), (uzf, uz_h), (dzf, dz_h)]:
                    gpsimd.dma_start(out=dst[:], in_=src).then_inc(sGP.h, 16)
                    sGP.n += 16
            scales_gp = sGP.n

            tr_cnt0 = sT.n
            ev_cnt0 = sSC.n
            dq_cnt0 = sV.n
            for dt in range(NDT):
                @block.sync
                def _(sync, dt=dt):
                    if dt >= 2:
                        sync.wait_ge(sV.h, dq_cnt0 + (dt - 1) * NST)
                    sync.dma_start(out=din[dt % 2][:], in_=dTt[dt]
                                   ).then_inc(sDN[dt % 2].h, 16)
                    sDN[dt % 2].n += 16

                @block.vector
                def _(vector, dt=dt):
                    if dt == 0:
                        vector.wait_ge(sGP.h, scales_gp)
                        vector.wait_ge(sIN.h, scales_in)
                    vector.wait_ge(sDN[dt % 2].h, (dt // 2 + 1) * 16)
                    if dt >= 2:
                        vector.wait_ge(sT.h, tr_cnt0 + ((dt - 1) * NST - 1) // 8 + 1)
                    for it in range(NST):
                        ti = vector.tensor_scalar(
                            out=ddq[dt % 2][:, it * 128:(it + 1) * 128],
                            in0=din[dt % 2][:, it * 128:(it + 1) * 128],
                            scalar1=dzf[:, dt, it:it + 1],
                            scalar2=dsf[:, dt, it:it + 1],
                            op0=OP.subtract, op1=OP.mult)
                        sV.inc(ti)

                @block.tensor
                def _(tensor, dt=dt):
                    for it in range(NST):
                        k = dt * NST + it
                        slot = k % 8
                        if slot == 0 and k >= 16:
                            tensor.wait_ge(sSC.h, ev_cnt0 + k // 8 - 1)
                        tensor.wait_ge(sV.h, dq_cnt0 + k + 1)
                        ti = tensor.transpose(
                            psT[(k // 8) % 2][:, slot * 128:(slot + 1) * 128],
                            ddq[dt % 2][:, it * 128:(it + 1) * 128], ident[:])
                        if slot == 7 or k == NDT * NST - 1:
                            sT.inc(ti)

            ngrp = NDT * NST // 8

            @block.scalar
            def _(scalar):
                for g in range(ngrp):
                    scalar.wait_ge(sT.h, tr_cnt0 + g + 1)
                    for j in range(8):
                        k = g * 8 + j
                        dt, it = divmod(k, NST)
                        ti = scalar.activation(
                            out=stgd[:, it, dt * 128:(dt + 1) * 128],
                            in_=psT[g % 2][:, j * 128:(j + 1) * 128], func=AF.Copy)
                    sSC.inc(ti)
            dn_ev_done = sSC.n

            @block.gpsimd
            def _(gpsimd):
                gpsimd.wait_ge(sSC.h, dn_ev_done)
                for s in range(NST):
                    gpsimd.dma_start(out=contrib[s].ap()[0, 2],
                                     in_=stgd[:, s, :]).then_inc(sCTD.h, 16)
                    sCTD.n += 16
            down_ct = sCTD.n

        xrA = ent(nc.sbuf_tensor("xrA", [128, NTTP, D], F32))

        with ExitStack() as pmid:
            xrB = pmid.enter_context(nc.sbuf_tensor("xrB", [128, NTTP, D], F32))

            def xr(tt):
                return xrA[:, tt, :] if tt < NTTP else xrB[:, tt - NTTP, :]

            @block.sync
            def _(sync):
                sync.wait_ge(sCTD.h, down_ct)
                sync.dma_start(out=xrA[:],
                               in_=x_in[0:512, :].rearrange("(a p) d -> p a d", p=128)
                               ).then_inc(sIN.h, 16)
                sIN.n += 16
                sync.dma_start(out=xrB[:],
                               in_=x_in[512:1024, :].rearrange("(a p) d -> p a d", p=128)
                               ).then_inc(sIN.h, 16)
                sIN.n += 16
            x_loaded = sIN.n

            with ExitStack() as p0b:
                gin = p0b.enter_context(nc.sbuf_tensor("gin", [128, D], I32))
                gdq = p0b.enter_context(nc.sbuf_tensor("gdq", [128, D], BF16))
                stgg = p0b.enter_context(nc.sbuf_tensor("stgg", [128, NDT, 128], BF16))
                stgu = p0b.enter_context(nc.sbuf_tensor("stgu", [128, NDT, 128], BF16))

                gu_in0 = sIN2.n
                gu_dq0 = sV.n
                gu_tr0 = sT.n
                gu_ev0 = sSC.n
                for s in range(NST):
                    for half in range(2):
                        k = 2 * s + half
                        win, wdq = gin, gdq
                        sf, zf = (gsf, gzf) if half == 0 else (usf, uzf)
                        stg = stgg if half == 0 else stgu

                        @block.scalar
                        def _(scalar, k=k, win=win):
                            if k == 0:
                                scalar.wait_ge(sCTD.h, down_ct)
                            if k >= 1:
                                scalar.wait_ge(sV.h, gu_dq0 + k * NDT)
                            scalar.dma_start(out=win[:], in_=gT[k // 2] if k % 2 == 0
                                             else uT[k // 2]).then_inc(sIN2.h, 16)
                            sIN2.n += 16

                        @block.vector
                        def _(vector, k=k, s=s, win=win, wdq=wdq, sf=sf, zf=zf):
                            vector.wait_ge(sIN2.h, gu_in0 + (k + 1) * 16)
                            if k >= 1:
                                vector.wait_ge(sT.h, gu_tr0 + k * 4)
                            for dt in range(NDT):
                                ti = vector.tensor_scalar(
                                    out=wdq[:, dt * 128:(dt + 1) * 128],
                                    in0=win[:, dt * 128:(dt + 1) * 128],
                                    scalar1=zf[:, s, dt:dt + 1],
                                    scalar2=sf[:, s, dt:dt + 1],
                                    op0=OP.subtract, op1=OP.mult)
                                sV.inc(ti)

                        @block.tensor
                        def _(tensor, k=k, wdq=wdq):
                            for grp in range(4):
                                gk = k * 4 + grp
                                if gk >= 2:
                                    tensor.wait_ge(sSC.h, gu_ev0 + gk - 1)
                                tensor.wait_ge(sV.h, gu_dq0 + k * NDT + (grp + 1) * 8)
                                for j in range(8):
                                    dt = grp * 8 + j
                                    ti = tensor.transpose(
                                        psT[gk % 2][:, j * 128:(j + 1) * 128],
                                        wdq[:, dt * 128:(dt + 1) * 128], ident[:])
                                sT.inc(ti)

                        @block.scalar
                        def _(scalar, k=k, s=s, half=half, stg=stg):
                            sCX = sCTG if half == 0 else sCTU
                            if s >= 1:
                                scalar.wait_ge(sCX.h, s * 16)
                            for grp in range(4):
                                gk = k * 4 + grp
                                scalar.wait_ge(sT.h, gu_tr0 + gk + 1)
                                ti = scalar.activation(
                                    out=stg[:, grp * 8:(grp + 1) * 8, :],
                                    in_=psT[gk % 2][:].rearrange(
                                        "p (a b) -> p a b", a=8),
                                    func=AF.Copy)
                                sSC.inc(ti)
                            scalar.wait_ge(sSC.h, sSC.n)
                            scalar.dma_start(
                                out=contrib[s].ap()[0, half],
                                in_=stg[:].rearrange("p a b -> p (a b)")
                                ).then_inc(sCX.h, 16)
                            sCX.n += 16

                for s in range(2):
                    @block.gpsimd
                    def _(gpsimd, s=s):
                        gpsimd.wait_ge(sCTD.h, down_ct)
                        gpsimd.wait_ge(sCTG.h, (s + 1) * 16)
                        gpsimd.wait_ge(sCTU.h, (s + 1) * 16)
                        gpsimd.collective_compute(
                            "AllGather", OP.bypass,
                            replica_groups=[list(range(NC))],
                            ins=[contrib[s].ap()], outs=[agout[s].ap()],
                        ).then_inc(sAG.h, 1)
                        sAG.n += 1

            def emit_ln_stats(nmean, rstd, wait_first=None):
                @block.vector
                def _(vector):
                    vector.wait_ge(sIN.h, x_loaded)
                    if wait_first is not None:
                        wait_first(vector)
                    for tt in range(NTT):
                        ti = vector.reduce_sum(out=ssum[:, tt:tt + 1],
                                               in_=xr(tt), axis=AX.X)
                        sV.inc(ti)
                red_done = sV.n

                @block.scalar
                def _(scalar):
                    scalar.wait_ge(sIN.h, x_loaded)
                    if wait_first is not None:
                        wait_first(scalar)
                    sq0 = sSC.n
                    for tt in range(NTT):
                        if tt >= 2:
                            scalar.wait_ge(sSC.h, sq0 + tt - 1)
                        ti = scalar.activation(out=trash[tt % 2][:], in_=xr(tt),
                                               func=AF.Square,
                                               accum_out=ssq[:, tt:tt + 1])
                        sSC.inc(ti)
                sq_done = sSC.n

                @block.vector
                def _(vector):
                    vector.wait_ge(sV.h, red_done)
                    vector.wait_ge(sSC.h, sq_done)
                    ti = vector.tensor_scalar(out=nmean[:], in0=ssum[:],
                                              scalar1=-1.0 / D, scalar2=None,
                                              op0=OP.mult)
                    sV.inc(ti)
                    vector.wait_ge(sV.h, sV.n)
                    ti = vector.scalar_tensor_tensor(
                        out=svar[:], in0=nmean[:], scalar=1.0, in1=nmean[:],
                        op0=OP.mult, op1=OP.mult)
                    sV.inc(ti)
                    vector.wait_ge(sV.h, sV.n)
                    ti = vector.scalar_tensor_tensor(
                        out=svar[:], in0=ssq[:], scalar=1.0 / D, in1=svar[:],
                        op0=OP.mult, op1=OP.subtract)
                    sV.inc(ti)
                    vector.wait_ge(sV.h, sV.n)
                    ti = vector.tensor_scalar(out=svar[:], in0=svar[:],
                                              scalar1=EPS, scalar2=None,
                                              op0=OP.add)
                    sV.inc(ti)
                var_done = sV.n

                @block.scalar
                def _(scalar):
                    scalar.wait_ge(sV.h, var_done)
                    ti = scalar.activation(out=ssd[:], in_=svar[:], func=AF.Sqrt)
                    sSC.inc(ti)
                sd_done = sSC.n

                @block.vector
                def _(vector):
                    vector.wait_ge(sSC.h, sd_done)
                    ti = vector.reciprocal(out=rstd[:], in_=ssd[:])
                    sV.inc(ti)

            def emit_build_hT(tts, nmean, rstd, target, tmp, col_of_tt,
                              xget, extra_first=None):
                for i, tt in enumerate(tts):
                    @block.vector
                    def _(vector, tt=tt, i=i):
                        if i == 0 and extra_first is not None:
                            extra_first(vector)
                        if i == 0:
                            vector.wait_ge(sV.h, sV.n)
                        if i >= 1:
                            vector.wait_ge(sT.h, sT.n)
                        ti = vector.tensor_scalar(
                            out=tmp[:], in0=xget(tt),
                            scalar1=nmean[:, tt:tt + 1],
                            scalar2=rstd[:, tt:tt + 1],
                            op0=OP.add, op1=OP.mult)
                        sV.inc(ti)

                    @block.tensor
                    def _(tensor):
                        tensor.wait_ge(sV.h, sV.n)
                        ev0 = sSC.n
                        for grp in range(4):
                            tensor.wait_ge(sSC.h, ev0 + grp - 1)
                            for j in range(8):
                                dt = grp * 8 + j
                                ti = tensor.transpose(
                                    psT[grp % 2][:, j * 128:(j + 1) * 128],
                                    tmp[:, dt * 128:(dt + 1) * 128], ident[:])
                            sT.inc(ti)

                    @block.scalar
                    def _(scalar, tt=tt):
                        col = col_of_tt(tt)
                        base_t = sT.n - 4
                        for grp in range(4):
                            scalar.wait_ge(sT.h, base_t + grp + 1)
                            ti = scalar.activation(
                                out=target[:, grp * 8:(grp + 1) * 8,
                                            col * 128:(col + 1) * 128],
                                in_=psT[grp % 2][:].rearrange(
                                    "p (a b) -> p a b", a=8),
                                func=AF.Copy)
                            sSC.inc(ti)

            with ExitStack() as pat:
                h1T = pat.enter_context(
                    nc.sbuf_tensor("h1T", [128, NDT, 512], BF16))
                htmp = pat.enter_context(nc.sbuf_tensor("htmp", [128, D], BF16))
                msga = [pat.enter_context(
                    nc.sbuf_tensor(f"msga{i}", [128, 512], BF16))
                    for i in range(2)]

                @block.vector
                def _(vector):
                    ti = vector.memzero(S0[:].rearrange("p a b -> p (a b)"))
                    sV.inc(ti)
                    ti = vector.memzero(S1[:].rearrange("p a b -> p (a b)"))
                    sV.inc(ti)

                emit_ln_stats(nmean1, rstd1,
                              wait_first=lambda e: (
                                  e.wait_ge(sCTG.h, NST * 16),
                                  e.wait_ge(sCTU.h, NST * 16)))
                half_tts = {H: [2 * n + H for n in range(4)] for H in (0, 1)}

                def emit_grams(Sdst):
                    @block.vector
                    def _(vector, Sdst=Sdst):
                        vector.wait_ge(sSC.h, sSC.n)
                        vector.wait_ge(sV.h, sV.n)
                        for dt in range(NDT):
                            for n in range(4):
                                for m in range(n, 4):
                                    j = len(gr_incs)
                                    if j >= GRK:
                                        vector.wait_ge(sV.h, gr_incs[j - GRK])
                                    slot = j % GRK
                                    ti = vector.scalar_tensor_tensor(
                                        out=grt[:, slot * 128:(slot + 1) * 128],
                                        in0=h1T[:, dt, n * 128:(n + 1) * 128],
                                        scalar=1.0,
                                        in1=h1T[:, dt, m * 128:(m + 1) * 128],
                                        op0=OP.mult, op1=OP.mult,
                                        accum_out=Sdst[:, dt,
                                                       n * 4 + m:n * 4 + m + 1])
                                    gr_incs.append(sV.inc(ti))

                emit_build_hT(half_tts[0], nmean1, rstd1, h1T, htmp,
                              lambda tt: tt // 2, xr)
                emit_grams(S0)
                emit_build_hT(half_tts[1], nmean1, rstd1, h1T, htmp,
                              lambda tt: tt // 2, xr)
                emit_grams(S1)

                @block.vector
                def _(vector):
                    vector.wait_ge(sV.h, sV.n)
                    ti = vector.tensor_tensor(
                        out=Ssc[:].rearrange("p a b -> p (a b)"),
                        in0=S0[:].rearrange("p a b -> p (a b)"),
                        in1=S1[:].rearrange("p a b -> p (a b)"), op=OP.add)
                    sV.inc(ti)
                    vector.wait_ge(sV.h, sV.n)
                    for n in range(4):
                        for m in range(n):
                            ti = vector.tensor_copy(
                                Ssc[:, :, n * 4 + m:n * 4 + m + 1],
                                Ssc[:, :, m * 4 + n:m * 4 + n + 1])
                            sV.inc(ti)
                sym_done = sV.n

                @block.scalar
                def _(scalar):
                    scalar.wait_ge(sV.h, sym_done)
                    scalar.dma_start(out=ar_in.ap(),
                                     in_=Ssc[:].rearrange("p a b -> p (a b)")
                                     ).then_inc(sIN2.h, 16)
                    sIN2.n += 16
                ar_staged = sIN2.n

                @block.gpsimd
                def _(gpsimd):
                    gpsimd.wait_ge(sIN2.h, ar_staged)
                    gpsimd.collective_compute(
                        "AllReduce", OP.add,
                        replica_groups=[list(range(NC))],
                        ins=[ar_in.ap()], outs=[ar_out.ap()],
                    ).then_inc(sAR.h, 1)
                    sAR.n += 1
                    gpsimd.wait_ge(sAR.h, sAR.n)
                    gpsimd.dma_start(out=Ssc[:].rearrange("p a b -> p (a b)"),
                                     in_=ar_out.ap()).then_inc(sGP.h, 16)
                    sGP.n += 16
                    for s in range(2, NST):
                        gpsimd.wait_ge(sCTG.h, (s + 1) * 16)
                        gpsimd.wait_ge(sCTU.h, (s + 1) * 16)
                        gpsimd.collective_compute(
                            "AllGather", OP.bypass,
                            replica_groups=[list(range(NC))],
                            ins=[contrib[s].ap()], outs=[agout[s].ap()],
                        ).then_inc(sAG.h, 1)
                        sAG.n += 1
                ar_done = scores_back = sGP.n

                @block.vector
                def _(vector):
                    vector.wait_ge(sGP.h, scores_back)
                    ti = vector.tensor_reduce(
                        out=NMX[:],
                        in_=Ssc[:].rearrange("p a b -> p (a b)").rearrange(
                            "p (k m) -> p k m", m=4),
                        axis=AX.X, op=OP.max, negate=True)
                    sV.inc(ti)
                    vector.wait_ge(sV.h, sV.n)
                    for dt in range(NDT):
                        for n in range(4):
                            kk = dt * 4 + n
                            ti = vector.tensor_scalar(
                                out=Ssc[:, dt, n * 4:(n + 1) * 4],
                                in0=Ssc[:, dt, n * 4:(n + 1) * 4],
                                scalar1=NMX[:, kk:kk + 1], scalar2=None,
                                op0=OP.add)
                            sV.inc(ti)
                sub_done = sV.n

                @block.scalar
                def _(scalar):
                    scalar.wait_ge(sV.h, sub_done)
                    ti = scalar.activation(
                        out=Ssc[:].rearrange("p a b -> p (a b)"),
                        in_=Ssc[:].rearrange("p a b -> p (a b)"),
                        func=AF.Exp, scale=ATT_SCALE)
                    sSC.inc(ti)
                exp_done = sSC.n

                @block.vector
                def _(vector):
                    vector.wait_ge(sSC.h, exp_done)
                    ti = vector.reduce_sum(
                        out=SSM[:],
                        in_=Ssc[:].rearrange("p a b -> p (a b)").rearrange(
                            "p (k m) -> p k m", m=4), axis=AX.X)
                    sV.inc(ti)
                    vector.wait_ge(sV.h, sV.n)
                    ti = vector.reciprocal(out=R4[:], in_=SSM[:])
                    sV.inc(ti)
                    vector.wait_ge(sV.h, sV.n)
                    for dt in range(NDT):
                        for n in range(4):
                            kk = dt * 4 + n
                            ti = vector.tensor_scalar(
                                out=Ssc[:, dt, n * 4:(n + 1) * 4],
                                in0=Ssc[:, dt, n * 4:(n + 1) * 4],
                                scalar1=R4[:, kk:kk + 1], scalar2=None,
                                op0=OP.mult)
                            sV.inc(ti)
                prob_done = sV.n

                def emit_msg(H):
                    for dt in range(NDT):
                        @block.vector
                        def _(vector, dt=dt):
                            if dt == 0:
                                vector.wait_ge(sV.h, prob_done)
                                vector.wait_ge(sSC.h, sSC.n)
                            vector.wait_ge(sT.h, sT.n)
                            for m in range(4):
                                for n in range(4):
                                    e_ap = Ssc[:, dt, n * 4 + m:n * 4 + m + 1]
                                    h_ap = h1T[:, dt, m * 128:(m + 1) * 128]
                                    o_ap = msga[dt % 2][:, n * 128:(n + 1) * 128]
                                    if m == 0:
                                        ti = vector.tensor_scalar(
                                            out=o_ap, in0=h_ap, scalar1=e_ap,
                                            scalar2=None, op0=OP.mult)
                                    else:
                                        ti = vector.scalar_tensor_tensor(
                                            out=o_ap, in0=h_ap, scalar=e_ap,
                                            in1=o_ap, op0=OP.mult, op1=OP.add)
                                sV.inc(ti)
                                if m < 3:
                                    vector.wait_ge(sV.h, sV.n)
                            vector.wait_ge(sV.h, sV.n)
                            for n in range(4):
                                ti = vector.tensor_tensor(
                                    out=msga[dt % 2][:, n * 128:(n + 1) * 128],
                                    in0=msga[dt % 2][:, n * 128:(n + 1) * 128],
                                    in1=h1T[:, dt, n * 128:(n + 1) * 128],
                                    op=OP.add)
                                sV.inc(ti)

                        @block.tensor
                        def _(tensor, dt=dt):
                            tensor.wait_ge(sV.h, sV.n)
                            if dt < 2:
                                tensor.wait_ge(sSC.h, sSC.n)
                            for n in range(4):
                                ti = tensor.transpose(
                                    psT[dt % 2][:, n * 128:(n + 1) * 128],
                                    msga[dt % 2][:, n * 128:(n + 1) * 128],
                                    ident[:])
                            sT.inc(ti)

                        @block.vector
                        def _(vector, dt=dt, H=H):
                            vector.wait_ge(sT.h, sT.n)
                            for n in range(4):
                                tt = 2 * n + H
                                ti = vector.tensor_tensor(
                                    out=xr(tt)[:, dt * 128:(dt + 1) * 128],
                                    in0=xr(tt)[:, dt * 128:(dt + 1) * 128],
                                    in1=psT[dt % 2][:, n * 128:(n + 1) * 128],
                                    op=OP.add)
                                sV.inc(ti)

                emit_msg(1)
                emit_build_hT(half_tts[0], nmean1, rstd1, h1T, htmp,
                              lambda tt: tt // 2, xr)
                emit_msg(0)

                msg_all_done = sV.n
                emit_ln_stats(nmean2, rstd2,
                              wait_first=lambda e: e.wait_ge(sV.h, msg_all_done))

                ln2_v = sV.n
                ln2_s = sSC.n

                @block.gpsimd
                def _(gpsimd):
                    gpsimd.wait_ge(sV.h, ln2_v)
                    gpsimd.wait_ge(sSC.h, ln2_s)
                    gpsimd.dma_start(out=park.ap(),
                                     in_=xrB[:].rearrange("p a d -> p (a d)")
                                     ).then_inc(sGP.h, 16)
                    sGP.n += 16
                parked = sGP.n

        with ExitStack() as pff:
            h2T = pff.enter_context(nc.sbuf_tensor("h2T", [128, NDT, TPASS], BF16))
            htm2 = pff.enter_context(nc.sbuf_tensor("htm2", [128, D], BF16))
            act = pff.enter_context(nc.sbuf_tensor("act", [128, JLS, TPASS], BF16))
            wg = [pff.enter_context(nc.sbuf_tensor(f"wg{i}", [128, NDT, 128], BF16))
                  for i in range(2)]
            wu = [pff.enter_context(nc.sbuf_tensor(f"wu{i}", [128, NDT, 128], BF16))
                  for i in range(2)]
            wd = [pff.enter_context(nc.sbuf_tensor(f"wd{i}", [128, JLS, 512], BF16))
                  for i in range(2)]
            gsl = [pff.enter_context(nc.sbuf_tensor(f"gsl{i}", [128, TPASS], BF16))
                   for i in range(2)]

            gmm = {0: 0, 1: 0}
            umm = {0: 0, 1: 0}
            silu_ev = {0: 0, 1: 0}
            act_ev = {0: 0, 1: 0}
            dmm = {0: 0, 1: 0}
            dev = {0: 0, 1: 0}
            wdn = {0: 0, 1: 0}
            act_free = [0]

            for ffp in range(2):
                pass_tts = list(range(NTTP)) if ffp == 0 else list(range(NTTP, NTT))

                if ffp == 1:
                    @block.sync
                    def _(sync):
                        sync.wait_ge(sV.h, sV.n)
                        sync.dma_start(
                            out=out[0:512, :].rearrange("(a p) d -> p a d", p=128),
                            in_=xrA[:]).then_inc(sST.h, 16)
                        sST.n += 16
                        sync.wait_ge(sST.h, sST.n)
                        sync.wait_ge(sGP.h, parked)
                        sync.dma_start(out=xrA[:].rearrange("p a d -> p (a d)"),
                                       in_=park.ap()).then_inc(sIN.h, 16)
                        sIN.n += 16
                    xb_back = sIN.n

                    def xr_p(tt):
                        return xrA[:, tt - NTTP, :]

                    def first_wait(e):
                        e.wait_ge(sIN.h, xb_back)
                else:
                    def xr_p(tt):
                        return xrA[:, tt, :]

                    gin_total = sIN2.n

                    def first_wait(e):
                        e.wait_ge(sGP.h, parked)
                        e.wait_ge(sCTG.h, NST * 16)
                        e.wait_ge(sCTU.h, NST * 16)
                        e.wait_ge(sIN2.h, gin_total)

                emit_build_hT(pass_tts, nmean2, rstd2, h2T, htm2,
                              lambda tt: tt % NTTP, xr_p, extra_first=first_wait)
                h2T_sc = sSC.n

                for s in range(NST):
                    for jl in range(JLS):
                        @block.sync
                        def _(sync, s=s, jl=jl, ffp=ffp):
                            if ffp == 0 and s == 0 and jl == 0:
                                sync.wait_ge(sGP.h, parked)
                                sync.wait_ge(sCTG.h, NST * 16)
                                sync.wait_ge(sCTU.h, NST * 16)
                            if jl == 0:
                                sync.wait_ge(sAG.h, s + 1)
                            sync.wait_ge(sT.h, gmm[jl % 2])
                            sync.dma_start(
                                out=wg[jl % 2][:],
                                in_=agout[s].ap()[jl, 0].rearrange(
                                    "p (a b) -> p a b", b=128)
                                ).then_inc(sWG[jl % 2].h, 16)
                            sWG[jl % 2].n += 16
                            sync.wait_ge(sT.h, umm[jl % 2])
                            sync.dma_start(
                                out=wu[jl % 2][:],
                                in_=agout[s].ap()[jl, 1].rearrange(
                                    "p (a b) -> p a b", b=128)
                                ).then_inc(sWU[jl % 2].h, 16)
                            sWU[jl % 2].n += 16
                        wgl = sWG[jl % 2].n
                        wul = sWU[jl % 2].n

                        @block.tensor
                        def _(tensor, jl=jl):
                            tensor.wait_ge(sWG[jl % 2].h, wgl)
                            tensor.wait_ge(sSC.h, h2T_sc)
                            tensor.wait_ge(sSC.h, silu_ev[jl % 2])
                            for dt in range(NDT):
                                ti = tensor.matmul(
                                    psG[jl % 2][:], wg[jl % 2][:, dt, :],
                                    h2T[:, dt, :], start=(dt == 0),
                                    stop=(dt == NDT - 1))
                            sT.inc(ti)
                            gmm[jl % 2] = sT.n
                            tensor.wait_ge(sWU[jl % 2].h, wul)
                            tensor.wait_ge(sV.h, act_ev[jl % 2])
                            for dt in range(NDT):
                                ti = tensor.matmul(
                                    psU[jl % 2][:], wu[jl % 2][:, dt, :],
                                    h2T[:, dt, :], start=(dt == 0),
                                    stop=(dt == NDT - 1))
                            sT.inc(ti)
                            umm[jl % 2] = sT.n

                        @block.scalar
                        def _(scalar, jl=jl):
                            scalar.wait_ge(sT.h, gmm[jl % 2])
                            scalar.wait_ge(sV.h, act_ev[jl % 2])
                            ti = scalar.activation(out=gsl[jl % 2][:],
                                                   in_=psG[jl % 2][:], func=AF.Silu)
                            silu_ev[jl % 2] = sSC.inc(ti)

                        @block.vector
                        def _(vector, jl=jl):
                            vector.wait_ge(sT.h, umm[jl % 2])
                            vector.wait_ge(sSC.h, silu_ev[jl % 2])
                            if jl == 0:
                                vector.wait_ge(sT.h, act_free[0])
                            ti = vector.scalar_tensor_tensor(
                                out=act[:, jl, :], in0=psU[jl % 2][:], scalar=1.0,
                                in1=gsl[jl % 2][:], op0=OP.mult, op1=OP.mult)
                            act_ev[jl % 2] = sV.inc(ti)
                    act_ready = sV.n

                    for q in range(8):
                        @block.scalar
                        def _(scalar, s=s, q=q):
                            if q == 0:
                                scalar.wait_ge(sAG.h, s + 1)
                            scalar.wait_ge(sT.h, wdn[q % 2])
                            scalar.dma_start(
                                out=wd[q % 2][:],
                                in_=agout[s].ap()[:, 2, :, q * 512:(q + 1) * 512]
                                .rearrange("r p d -> p r d")
                                ).then_inc(sWD2[q % 2].h, 16)
                            sWD2[q % 2].n += 16
                        wdl = sWD2[q % 2].n

                        for tt_i in range(NTTP):
                            g = q * NTTP + tt_i

                            @block.tensor
                            def _(tensor, q=q, tt_i=tt_i, g=g):
                                tensor.wait_ge(sWD2[q % 2].h, wdl)
                                if g == 0:
                                    tensor.wait_ge(sV.h, act_ready)
                                tensor.wait_ge(sV.h, dev[g % 2])
                                for jl in range(JLS):
                                    ti = tensor.matmul(
                                        psD[g % 2][:],
                                        act[:, jl, tt_i * 128:(tt_i + 1) * 128],
                                        wd[q % 2][:, jl, :],
                                        start=(jl == 0), stop=(jl == JLS - 1))
                                sT.inc(ti)
                                dmm[g % 2] = sT.n
                                if tt_i == NTTP - 1:
                                    wdn[q % 2] = sT.n

                            @block.vector
                            def _(vector, q=q, tt_i=tt_i, g=g):
                                vector.wait_ge(sT.h, dmm[g % 2])
                                tt = pass_tts[tt_i]
                                ti = vector.tensor_tensor(
                                    out=xr_p(tt)[:, q * 512:(q + 1) * 512],
                                    in0=xr_p(tt)[:, q * 512:(q + 1) * 512],
                                    in1=psD[g % 2][:], op=OP.add)
                                dev[g % 2] = sV.inc(ti)
                    act_free[0] = sT.n

                if ffp == 1:
                    @block.sync
                    def _(sync):
                        sync.wait_ge(sV.h, sV.n)
                        sync.dma_start(
                            out=out[512:1024, :].rearrange("(a p) d -> p a d", p=128),
                            in_=xrA[:]).then_inc(sST.h, 16)
                        sST.n += 16
                        sync.wait_ge(sST.h, sST.n)

    return nc


def kernel(L0_attn_x, L0_ffn_gate_w, L0_ffn_gate_z, L0_ffn_gate_s,
           L0_ffn_up_w, L0_ffn_up_z, L0_ffn_up_s,
           L0_ffn_down_w, L0_ffn_down_z, L0_ffn_down_s):
    x = np.asarray(L0_attn_x, dtype=np.float32)
    gw = np.asarray(L0_ffn_gate_w, dtype=np.int32)
    gz = np.asarray(L0_ffn_gate_z, dtype=np.int32)
    gs = np.asarray(L0_ffn_gate_s, dtype=np.float32)
    uw = np.asarray(L0_ffn_up_w, dtype=np.int32)
    uz = np.asarray(L0_ffn_up_z, dtype=np.int32)
    us = np.asarray(L0_ffn_up_s, dtype=np.float32)
    dw = np.asarray(L0_ffn_down_w, dtype=np.int32)
    dz = np.asarray(L0_ffn_down_z, dtype=np.int32)
    ds = np.asarray(L0_ffn_down_s, dtype=np.float32)

    gwT = np.ascontiguousarray(gw.T)
    uwT = np.ascontiguousarray(uw.T)
    dwT = np.ascontiguousarray(dw.T)
    gsT = np.ascontiguousarray(gs.T)
    gzT = np.ascontiguousarray(gz.T)
    usT = np.ascontiguousarray(us.T)
    uzT = np.ascontiguousarray(uz.T)
    dsT = np.ascontiguousarray(ds.T)
    dzT = np.ascontiguousarray(dz.T)
    ident = np.eye(128, dtype=np.float32).astype(BF)

    nc = build_nc()
    in_maps = []
    for c in range(NC):
        jsel = np.concatenate(
            [np.arange(s * 1024 + c * 128, s * 1024 + (c + 1) * 128)
             for s in range(NST)])
        gidx = np.array([s * 8 + c for s in range(NST)])
        in_maps.append(dict(
            x_in=np.ascontiguousarray(x[:, c * SLOC:(c + 1) * SLOC, :]
                                      .reshape(T, D)),
            gT=np.ascontiguousarray(gwT[jsel].reshape(NST, 128, D)),
            uT=np.ascontiguousarray(uwT[jsel].reshape(NST, 128, D)),
            dTt=np.ascontiguousarray(dwT[:, jsel].reshape(NDT, 128, NST, 128)),
            gs_h=np.ascontiguousarray(gsT[jsel].reshape(NST, 128, 32)
                                      .transpose(1, 0, 2)),
            gz_h=np.ascontiguousarray(gzT[jsel].reshape(NST, 128, 32)
                                      .transpose(1, 0, 2)),
            us_h=np.ascontiguousarray(usT[jsel].reshape(NST, 128, 32)
                                      .transpose(1, 0, 2)),
            uz_h=np.ascontiguousarray(uzT[jsel].reshape(NST, 128, 32)
                                      .transpose(1, 0, 2)),
            ds_h=np.ascontiguousarray(dsT[:, gidx].reshape(NDT, 128, NST)
                                      .transpose(1, 0, 2)),
            dz_h=np.ascontiguousarray(dzT[:, gidx].reshape(NDT, 128, NST)
                                      .transpose(1, 0, 2)),
            idb=ident,
        ))

    res = run_bass_kernel_spmd(nc, in_maps, core_ids=list(range(NC)))

    outp = np.empty((B, S, D), dtype=np.float32)
    for c in range(NC):
        outp[:, c * SLOC:(c + 1) * SLOC, :] = \
            res.results[c]["out"].reshape(B, SLOC, D)
    return outp


# version 05
# speedup vs baseline: 1.1051x; 1.1051x over previous
import math
from contextlib import ExitStack

import numpy as np
import ml_dtypes

import concourse.bass as bass
import concourse.mybir as mybir
from concourse.bass_utils import run_bass_kernel_spmd

F32 = mybir.dt.float32
BF16 = mybir.dt.bfloat16
I32 = mybir.dt.int32
AF = mybir.ActivationFunctionType
OP = mybir.AluOpType
AX = mybir.AxisListType
BF = ml_dtypes.bfloat16

D = 4096
II = 14336
B = 4
S = 2048
NC = 8
SLOC = S // NC
T = B * SLOC
NTT = T // 128
NDT = D // 128
NST = 14
JLS = 8
TPASS = 512
NTTP = TPASS // 128
EPS = 1e-5
ATT_SCALE = 1.0 / math.sqrt(float(S))


class Ctr:
    def __init__(self, stack, nc, name):
        self.h = stack.enter_context(nc.semaphore(name))
        self.n = 0

    def inc(self, instr, k=1):
        instr.then_inc(self.h, k)
        self.n += k
        return self.n


def build_nc():
    nc = bass.Bass("TRN2", target_bir_lowering=False, debug=False)

    x_in = nc.dram_tensor("x_in", [T, D], F32, kind="ExternalInput").ap()
    gT = nc.dram_tensor("gT", [NST, 128, D], I32, kind="ExternalInput").ap()
    uT = nc.dram_tensor("uT", [NST, 128, D], I32, kind="ExternalInput").ap()
    dTt = nc.dram_tensor("dTt", [NDT, 128, NST, 128], I32, kind="ExternalInput").ap()
    gs_h = nc.dram_tensor("gs_h", [128, NST, NDT], F32, kind="ExternalInput").ap()
    gz_h = nc.dram_tensor("gz_h", [128, NST, NDT], I32, kind="ExternalInput").ap()
    us_h = nc.dram_tensor("us_h", [128, NST, NDT], F32, kind="ExternalInput").ap()
    uz_h = nc.dram_tensor("uz_h", [128, NST, NDT], I32, kind="ExternalInput").ap()
    ds_h = nc.dram_tensor("ds_h", [128, NDT, NST], F32, kind="ExternalInput").ap()
    dz_h = nc.dram_tensor("dz_h", [128, NDT, NST], I32, kind="ExternalInput").ap()
    idb = nc.dram_tensor("idb", [128, 128], BF16, kind="ExternalInput").ap()

    out = nc.dram_tensor("out", [T, D], F32, kind="ExternalOutput").ap()

    contrib = [nc.dram_tensor(f"ctb{s}", [1, 3, 128, D], BF16) for s in range(NST)]
    agout = [nc.dram_tensor(f"ago{s}", [NC, 3, 128, D], BF16,
                            addr_space="Shared") for s in range(NST)]
    ar_in = nc.dram_tensor("ar_in", [128, 512], F32)
    ar_out = nc.dram_tensor("ar_out", [128, 512], F32)
    park = nc.dram_tensor("park", [128, NTTP * D], F32)

    with ExitStack() as ctx:
        ent = ctx.enter_context

        psG = [ent(nc.psum_tensor(f"psG{i}", [128, 512], F32)) for i in range(2)]
        psU = [ent(nc.psum_tensor(f"psU{i}", [128, 512], F32)) for i in range(2)]
        psD = [ent(nc.psum_tensor(f"psD{i}", [128, 512], F32)) for i in range(2)]
        psT = [ent(nc.psum_tensor(f"psT{i}", [128, 1024], BF16)) for i in range(2)]

        sIN = Ctr(ctx, nc, "sIN")
        sIN2 = Ctr(ctx, nc, "sIN2")
        sGP = Ctr(ctx, nc, "sGP")
        sCTD = Ctr(ctx, nc, "sCTD")
        sCTG = Ctr(ctx, nc, "sCTG")
        sCTU = Ctr(ctx, nc, "sCTU")
        sAG = Ctr(ctx, nc, "sAG")
        sAR = Ctr(ctx, nc, "sAR")
        sT = Ctr(ctx, nc, "sT")
        sV = Ctr(ctx, nc, "sV")
        sSC = Ctr(ctx, nc, "sSC")
        sDN = [Ctr(ctx, nc, f"sDN{i}") for i in range(2)]
        sWG = [Ctr(ctx, nc, f"sWG{i}") for i in range(2)]
        sWU = [Ctr(ctx, nc, f"sWU{i}") for i in range(2)]
        sWD2 = [Ctr(ctx, nc, f"sWD{i}") for i in range(2)]
        sST = Ctr(ctx, nc, "sST")

        ident = ent(nc.sbuf_tensor("ident", [128, 128], BF16))
        nmean1 = ent(nc.sbuf_tensor("nmean1", [128, NTT], F32))
        rstd1 = ent(nc.sbuf_tensor("rstd1", [128, NTT], F32))
        nmean2 = ent(nc.sbuf_tensor("nmean2", [128, NTT], F32))
        rstd2 = ent(nc.sbuf_tensor("rstd2", [128, NTT], F32))
        ssum = ent(nc.sbuf_tensor("ssum", [128, NTT], F32))
        ssq = ent(nc.sbuf_tensor("ssq", [128, NTT], F32))
        svar = ent(nc.sbuf_tensor("svar", [128, NTT], F32))
        ssd = ent(nc.sbuf_tensor("ssd", [128, NTT], F32))
        S0 = ent(nc.sbuf_tensor("S0", [128, NDT, 16], F32))
        S1 = ent(nc.sbuf_tensor("S1", [128, NDT, 16], F32))
        Ssc = ent(nc.sbuf_tensor("Ssc", [128, NDT, 16], F32))
        NMX = ent(nc.sbuf_tensor("NMX", [128, NDT * 4], F32))
        SSM = ent(nc.sbuf_tensor("SSM", [128, NDT * 4], F32))
        R4 = ent(nc.sbuf_tensor("R4", [128, NDT * 4], F32))
        GRK = 8
        grt = ent(nc.sbuf_tensor("grt", [128, GRK * 128], F32))
        gr_incs = []
        trash = [ent(nc.sbuf_tensor(f"trash{i}", [128, D], BF16))
                 for i in range(2)]
        gsf = ent(nc.sbuf_tensor("gsf", [128, NST, NDT], F32))
        gzf = ent(nc.sbuf_tensor("gzf", [128, NST, NDT], F32))
        usf = ent(nc.sbuf_tensor("usf", [128, NST, NDT], F32))
        uzf = ent(nc.sbuf_tensor("uzf", [128, NST, NDT], F32))

        block = ent(nc.Block())

        with ExitStack() as p0:
            dsf = p0.enter_context(nc.sbuf_tensor("dsf", [128, NDT, NST], F32))
            dzf = p0.enter_context(nc.sbuf_tensor("dzf", [128, NDT, NST], F32))
            din = [p0.enter_context(nc.sbuf_tensor(f"din{i}", [128, NST * 128], I32))
                   for i in range(2)]
            ddq = [p0.enter_context(nc.sbuf_tensor(f"ddq{i}", [128, NST * 128], BF16))
                   for i in range(2)]
            stgd = p0.enter_context(nc.sbuf_tensor("stgd", [128, NST, D], BF16))

            @block.sync
            def _(sync):
                for dst, src in [(ident, idb), (gsf[:], gs_h), (usf[:], us_h),
                                 (dsf[:], ds_h)]:
                    sync.dma_start(out=dst if isinstance(dst, bass.AP) else dst[:],
                                   in_=src).then_inc(sIN.h, 16)
                    sIN.n += 16
            scales_in = sIN.n

            @block.gpsimd
            def _(gpsimd):
                for dst, src in [(gzf, gz_h), (uzf, uz_h), (dzf, dz_h)]:
                    gpsimd.dma_start(out=dst[:], in_=src).then_inc(sGP.h, 16)
                    sGP.n += 16
            scales_gp = sGP.n

            tr_cnt0 = sT.n
            ev_cnt0 = sSC.n
            dq_cnt0 = sV.n
            for dt in range(NDT):
                @block.sync
                def _(sync, dt=dt):
                    if dt >= 2:
                        sync.wait_ge(sV.h, dq_cnt0 + (dt - 1) * NST)
                    sync.dma_start(out=din[dt % 2][:], in_=dTt[dt]
                                   ).then_inc(sDN[dt % 2].h, 16)
                    sDN[dt % 2].n += 16

                @block.vector
                def _(vector, dt=dt):
                    if dt == 0:
                        vector.wait_ge(sGP.h, scales_gp)
                        vector.wait_ge(sIN.h, scales_in)
                    vector.wait_ge(sDN[dt % 2].h, (dt // 2 + 1) * 16)
                    if dt >= 2:
                        vector.wait_ge(sT.h, tr_cnt0 + ((dt - 1) * NST - 1) // 8 + 1)
                    for it in range(NST):
                        ti = vector.tensor_scalar(
                            out=ddq[dt % 2][:, it * 128:(it + 1) * 128],
                            in0=din[dt % 2][:, it * 128:(it + 1) * 128],
                            scalar1=dzf[:, dt, it:it + 1],
                            scalar2=dsf[:, dt, it:it + 1],
                            op0=OP.subtract, op1=OP.mult)
                        sV.inc(ti)

                @block.tensor
                def _(tensor, dt=dt):
                    for it in range(NST):
                        k = dt * NST + it
                        slot = k % 8
                        if slot == 0 and k >= 16:
                            tensor.wait_ge(sSC.h, ev_cnt0 + k // 8 - 1)
                        tensor.wait_ge(sV.h, dq_cnt0 + k + 1)
                        ti = tensor.transpose(
                            psT[(k // 8) % 2][:, slot * 128:(slot + 1) * 128],
                            ddq[dt % 2][:, it * 128:(it + 1) * 128], ident[:])
                        if slot == 7 or k == NDT * NST - 1:
                            sT.inc(ti)

            ngrp = NDT * NST // 8

            @block.scalar
            def _(scalar):
                for g in range(ngrp):
                    scalar.wait_ge(sT.h, tr_cnt0 + g + 1)
                    for j in range(8):
                        k = g * 8 + j
                        dt, it = divmod(k, NST)
                        ti = scalar.activation(
                            out=stgd[:, it, dt * 128:(dt + 1) * 128],
                            in_=psT[g % 2][:, j * 128:(j + 1) * 128], func=AF.Copy)
                    sSC.inc(ti)
            dn_ev_done = sSC.n

            @block.gpsimd
            def _(gpsimd):
                gpsimd.wait_ge(sSC.h, dn_ev_done)
                for s in range(NST):
                    gpsimd.dma_start(out=contrib[s].ap()[0, 2],
                                     in_=stgd[:, s, :]).then_inc(sCTD.h, 16)
                    sCTD.n += 16
            down_ct = sCTD.n

        xrA = ent(nc.sbuf_tensor("xrA", [128, NTTP, D], F32))

        with ExitStack() as pmid:
            xrB = pmid.enter_context(nc.sbuf_tensor("xrB", [128, NTTP, D], F32))

            def xr(tt):
                return xrA[:, tt, :] if tt < NTTP else xrB[:, tt - NTTP, :]

            @block.sync
            def _(sync):
                sync.wait_ge(sCTD.h, down_ct)
                sync.dma_start(out=xrA[:],
                               in_=x_in[0:512, :].rearrange("(a p) d -> p a d", p=128)
                               ).then_inc(sIN.h, 16)
                sIN.n += 16
                sync.dma_start(out=xrB[:],
                               in_=x_in[512:1024, :].rearrange("(a p) d -> p a d", p=128)
                               ).then_inc(sIN.h, 16)
                sIN.n += 16
            x_loaded = sIN.n

            with ExitStack() as p0b:
                gin = p0b.enter_context(nc.sbuf_tensor("gin", [128, D], I32))
                gdq = p0b.enter_context(nc.sbuf_tensor("gdq", [128, D], BF16))
                stgg = p0b.enter_context(nc.sbuf_tensor("stgg", [128, NDT, 128], BF16))
                stgu = p0b.enter_context(nc.sbuf_tensor("stgu", [128, NDT, 128], BF16))

                gu_in0 = sIN2.n
                gu_dq0 = sV.n
                gu_tr0 = sT.n
                gu_ev0 = sSC.n
                for s in range(NST):
                    for half in range(2):
                        k = 2 * s + half
                        win, wdq = gin, gdq
                        sf, zf = (gsf, gzf) if half == 0 else (usf, uzf)
                        stg = stgg if half == 0 else stgu

                        @block.scalar
                        def _(scalar, k=k, win=win):
                            if k == 0:
                                scalar.wait_ge(sCTD.h, down_ct)
                            if k >= 1:
                                scalar.wait_ge(sV.h, gu_dq0 + k * NDT)
                            scalar.dma_start(out=win[:], in_=gT[k // 2] if k % 2 == 0
                                             else uT[k // 2]).then_inc(sIN2.h, 16)
                            sIN2.n += 16

                        @block.vector
                        def _(vector, k=k, s=s, win=win, wdq=wdq, sf=sf, zf=zf):
                            vector.wait_ge(sIN2.h, gu_in0 + (k + 1) * 16)
                            if k >= 1:
                                vector.wait_ge(sT.h, gu_tr0 + k * 4)
                            for dt in range(NDT):
                                ti = vector.tensor_scalar(
                                    out=wdq[:, dt * 128:(dt + 1) * 128],
                                    in0=win[:, dt * 128:(dt + 1) * 128],
                                    scalar1=zf[:, s, dt:dt + 1],
                                    scalar2=sf[:, s, dt:dt + 1],
                                    op0=OP.subtract, op1=OP.mult)
                                sV.inc(ti)

                        @block.tensor
                        def _(tensor, k=k, wdq=wdq):
                            for grp in range(4):
                                gk = k * 4 + grp
                                if gk >= 2:
                                    tensor.wait_ge(sSC.h, gu_ev0 + gk - 1)
                                tensor.wait_ge(sV.h, gu_dq0 + k * NDT + (grp + 1) * 8)
                                for j in range(8):
                                    dt = grp * 8 + j
                                    ti = tensor.transpose(
                                        psT[gk % 2][:, j * 128:(j + 1) * 128],
                                        wdq[:, dt * 128:(dt + 1) * 128], ident[:])
                                sT.inc(ti)

                        @block.scalar
                        def _(scalar, k=k, s=s, half=half, stg=stg):
                            sCX = sCTG if half == 0 else sCTU
                            if s >= 1:
                                scalar.wait_ge(sCX.h, s * 16)
                            for grp in range(4):
                                gk = k * 4 + grp
                                scalar.wait_ge(sT.h, gu_tr0 + gk + 1)
                                ti = scalar.activation(
                                    out=stg[:, grp * 8:(grp + 1) * 8, :],
                                    in_=psT[gk % 2][:].rearrange(
                                        "p (a b) -> p a b", a=8),
                                    func=AF.Copy)
                                sSC.inc(ti)
                            scalar.wait_ge(sSC.h, sSC.n)
                            scalar.dma_start(
                                out=contrib[s].ap()[0, half],
                                in_=stg[:].rearrange("p a b -> p (a b)")
                                ).then_inc(sCX.h, 16)
                            sCX.n += 16

                for s in range(2):
                    @block.gpsimd
                    def _(gpsimd, s=s):
                        gpsimd.wait_ge(sCTD.h, down_ct)
                        gpsimd.wait_ge(sCTG.h, (s + 1) * 16)
                        gpsimd.wait_ge(sCTU.h, (s + 1) * 16)
                        gpsimd.collective_compute(
                            "AllGather", OP.bypass,
                            replica_groups=[list(range(NC))],
                            ins=[contrib[s].ap()], outs=[agout[s].ap()],
                        ).then_inc(sAG.h, 1)
                        sAG.n += 1

            def emit_ln_stats(nmean, rstd, wait_first=None):
                @block.vector
                def _(vector):
                    vector.wait_ge(sIN.h, x_loaded)
                    if wait_first is not None:
                        wait_first(vector)
                    for tt in range(NTT):
                        ti = vector.reduce_sum(out=ssum[:, tt:tt + 1],
                                               in_=xr(tt), axis=AX.X)
                        sV.inc(ti)
                red_done = sV.n

                @block.scalar
                def _(scalar):
                    scalar.wait_ge(sIN.h, x_loaded)
                    if wait_first is not None:
                        wait_first(scalar)
                    sq0 = sSC.n
                    for tt in range(NTT):
                        if tt >= 2:
                            scalar.wait_ge(sSC.h, sq0 + tt - 1)
                        ti = scalar.activation(out=trash[tt % 2][:], in_=xr(tt),
                                               func=AF.Square,
                                               accum_out=ssq[:, tt:tt + 1])
                        sSC.inc(ti)
                sq_done = sSC.n

                @block.vector
                def _(vector):
                    vector.wait_ge(sV.h, red_done)
                    vector.wait_ge(sSC.h, sq_done)
                    ti = vector.tensor_scalar(out=nmean[:], in0=ssum[:],
                                              scalar1=-1.0 / D, scalar2=None,
                                              op0=OP.mult)
                    sV.inc(ti)
                    vector.wait_ge(sV.h, sV.n)
                    ti = vector.scalar_tensor_tensor(
                        out=svar[:], in0=nmean[:], scalar=1.0, in1=nmean[:],
                        op0=OP.mult, op1=OP.mult)
                    sV.inc(ti)
                    vector.wait_ge(sV.h, sV.n)
                    ti = vector.scalar_tensor_tensor(
                        out=svar[:], in0=ssq[:], scalar=1.0 / D, in1=svar[:],
                        op0=OP.mult, op1=OP.subtract)
                    sV.inc(ti)
                    vector.wait_ge(sV.h, sV.n)
                    ti = vector.tensor_scalar(out=svar[:], in0=svar[:],
                                              scalar1=EPS, scalar2=None,
                                              op0=OP.add)
                    sV.inc(ti)
                var_done = sV.n

                @block.scalar
                def _(scalar):
                    scalar.wait_ge(sV.h, var_done)
                    ti = scalar.activation(out=ssd[:], in_=svar[:], func=AF.Sqrt)
                    sSC.inc(ti)
                sd_done = sSC.n

                @block.vector
                def _(vector):
                    vector.wait_ge(sSC.h, sd_done)
                    ti = vector.reciprocal(out=rstd[:], in_=ssd[:])
                    sV.inc(ti)

            def emit_build_hT(tts, nmean, rstd, target, tmp, col_of_tt,
                              xget, extra_first=None):
                for i, tt in enumerate(tts):
                    @block.vector
                    def _(vector, tt=tt, i=i):
                        if i == 0 and extra_first is not None:
                            extra_first(vector)
                        if i == 0:
                            vector.wait_ge(sV.h, sV.n)
                        if i >= 1:
                            vector.wait_ge(sT.h, sT.n)
                        ti = vector.tensor_scalar(
                            out=tmp[:], in0=xget(tt),
                            scalar1=nmean[:, tt:tt + 1],
                            scalar2=rstd[:, tt:tt + 1],
                            op0=OP.add, op1=OP.mult)
                        sV.inc(ti)

                    @block.tensor
                    def _(tensor):
                        tensor.wait_ge(sV.h, sV.n)
                        ev0 = sSC.n
                        for grp in range(4):
                            tensor.wait_ge(sSC.h, ev0 + grp - 1)
                            for j in range(8):
                                dt = grp * 8 + j
                                ti = tensor.transpose(
                                    psT[grp % 2][:, j * 128:(j + 1) * 128],
                                    tmp[:, dt * 128:(dt + 1) * 128], ident[:])
                            sT.inc(ti)

                    @block.scalar
                    def _(scalar, tt=tt):
                        col = col_of_tt(tt)
                        base_t = sT.n - 4
                        for grp in range(4):
                            scalar.wait_ge(sT.h, base_t + grp + 1)
                            ti = scalar.activation(
                                out=target[:, grp * 8:(grp + 1) * 8,
                                            col * 128:(col + 1) * 128],
                                in_=psT[grp % 2][:].rearrange(
                                    "p (a b) -> p a b", a=8),
                                func=AF.Copy)
                            sSC.inc(ti)

            with ExitStack() as pat:
                h1T = pat.enter_context(
                    nc.sbuf_tensor("h1T", [128, NDT, 512], BF16))
                htmp = pat.enter_context(nc.sbuf_tensor("htmp", [128, D], BF16))
                msga = [pat.enter_context(
                    nc.sbuf_tensor(f"msga{i}", [128, 512], BF16))
                    for i in range(2)]

                @block.vector
                def _(vector):
                    ti = vector.memzero(S0[:].rearrange("p a b -> p (a b)"))
                    sV.inc(ti)
                    ti = vector.memzero(S1[:].rearrange("p a b -> p (a b)"))
                    sV.inc(ti)

                emit_ln_stats(nmean1, rstd1,
                              wait_first=lambda e: (
                                  e.wait_ge(sCTG.h, NST * 16),
                                  e.wait_ge(sCTU.h, NST * 16)))
                half_tts = {H: [2 * n + H for n in range(4)] for H in (0, 1)}

                def emit_grams(Sdst):
                    @block.vector
                    def _(vector, Sdst=Sdst):
                        vector.wait_ge(sSC.h, sSC.n)
                        vector.wait_ge(sV.h, sV.n)
                        for dt in range(NDT):
                            for n in range(4):
                                for m in range(n, 4):
                                    j = len(gr_incs)
                                    if j >= GRK:
                                        vector.wait_ge(sV.h, gr_incs[j - GRK])
                                    slot = j % GRK
                                    ti = vector.scalar_tensor_tensor(
                                        out=grt[:, slot * 128:(slot + 1) * 128],
                                        in0=h1T[:, dt, n * 128:(n + 1) * 128],
                                        scalar=1.0,
                                        in1=h1T[:, dt, m * 128:(m + 1) * 128],
                                        op0=OP.mult, op1=OP.mult,
                                        accum_out=Sdst[:, dt,
                                                       n * 4 + m:n * 4 + m + 1])
                                    gr_incs.append(sV.inc(ti))

                emit_build_hT(half_tts[0], nmean1, rstd1, h1T, htmp,
                              lambda tt: tt // 2, xr)
                emit_grams(S0)
                emit_build_hT(half_tts[1], nmean1, rstd1, h1T, htmp,
                              lambda tt: tt // 2, xr)
                emit_grams(S1)

                @block.vector
                def _(vector):
                    vector.wait_ge(sV.h, sV.n)
                    ti = vector.tensor_tensor(
                        out=Ssc[:].rearrange("p a b -> p (a b)"),
                        in0=S0[:].rearrange("p a b -> p (a b)"),
                        in1=S1[:].rearrange("p a b -> p (a b)"), op=OP.add)
                    sV.inc(ti)
                    vector.wait_ge(sV.h, sV.n)
                    for n in range(4):
                        for m in range(n):
                            ti = vector.tensor_copy(
                                Ssc[:, :, n * 4 + m:n * 4 + m + 1],
                                Ssc[:, :, m * 4 + n:m * 4 + n + 1])
                            sV.inc(ti)
                sym_done = sV.n

                @block.scalar
                def _(scalar):
                    scalar.wait_ge(sV.h, sym_done)
                    scalar.dma_start(out=ar_in.ap(),
                                     in_=Ssc[:].rearrange("p a b -> p (a b)")
                                     ).then_inc(sIN2.h, 16)
                    sIN2.n += 16
                ar_staged = sIN2.n

                @block.gpsimd
                def _(gpsimd):
                    gpsimd.wait_ge(sIN2.h, ar_staged)
                    gpsimd.collective_compute(
                        "AllReduce", OP.add,
                        replica_groups=[list(range(NC))],
                        ins=[ar_in.ap()], outs=[ar_out.ap()],
                    ).then_inc(sAR.h, 1)
                    sAR.n += 1
                    gpsimd.wait_ge(sAR.h, sAR.n)
                    gpsimd.dma_start(out=Ssc[:].rearrange("p a b -> p (a b)"),
                                     in_=ar_out.ap()).then_inc(sGP.h, 16)
                    sGP.n += 16
                    for s in range(2, NST):
                        gpsimd.wait_ge(sCTG.h, (s + 1) * 16)
                        gpsimd.wait_ge(sCTU.h, (s + 1) * 16)
                        gpsimd.collective_compute(
                            "AllGather", OP.bypass,
                            replica_groups=[list(range(NC))],
                            ins=[contrib[s].ap()], outs=[agout[s].ap()],
                        ).then_inc(sAG.h, 1)
                        sAG.n += 1
                ar_done = scores_back = sGP.n

                @block.vector
                def _(vector):
                    vector.wait_ge(sGP.h, scores_back)
                    ti = vector.tensor_reduce(
                        out=NMX[:],
                        in_=Ssc[:].rearrange("p a b -> p (a b)").rearrange(
                            "p (k m) -> p k m", m=4),
                        axis=AX.X, op=OP.max, negate=True)
                    sV.inc(ti)
                    vector.wait_ge(sV.h, sV.n)
                    for dt in range(NDT):
                        for n in range(4):
                            kk = dt * 4 + n
                            ti = vector.tensor_scalar(
                                out=Ssc[:, dt, n * 4:(n + 1) * 4],
                                in0=Ssc[:, dt, n * 4:(n + 1) * 4],
                                scalar1=NMX[:, kk:kk + 1], scalar2=None,
                                op0=OP.add)
                            sV.inc(ti)
                sub_done = sV.n

                @block.scalar
                def _(scalar):
                    scalar.wait_ge(sV.h, sub_done)
                    ti = scalar.activation(
                        out=Ssc[:].rearrange("p a b -> p (a b)"),
                        in_=Ssc[:].rearrange("p a b -> p (a b)"),
                        func=AF.Exp, scale=ATT_SCALE)
                    sSC.inc(ti)
                exp_done = sSC.n

                @block.vector
                def _(vector):
                    vector.wait_ge(sSC.h, exp_done)
                    ti = vector.reduce_sum(
                        out=SSM[:],
                        in_=Ssc[:].rearrange("p a b -> p (a b)").rearrange(
                            "p (k m) -> p k m", m=4), axis=AX.X)
                    sV.inc(ti)
                    vector.wait_ge(sV.h, sV.n)
                    ti = vector.reciprocal(out=R4[:], in_=SSM[:])
                    sV.inc(ti)
                    vector.wait_ge(sV.h, sV.n)
                    for dt in range(NDT):
                        for n in range(4):
                            kk = dt * 4 + n
                            ti = vector.tensor_scalar(
                                out=Ssc[:, dt, n * 4:(n + 1) * 4],
                                in0=Ssc[:, dt, n * 4:(n + 1) * 4],
                                scalar1=R4[:, kk:kk + 1], scalar2=None,
                                op0=OP.mult)
                            sV.inc(ti)
                prob_done = sV.n

                def emit_msg(H):
                    for dt in range(NDT):
                        @block.vector
                        def _(vector, dt=dt):
                            if dt == 0:
                                vector.wait_ge(sV.h, prob_done)
                                vector.wait_ge(sSC.h, sSC.n)
                            vector.wait_ge(sT.h, sT.n)
                            for m in range(4):
                                for n in range(4):
                                    e_ap = Ssc[:, dt, n * 4 + m:n * 4 + m + 1]
                                    h_ap = h1T[:, dt, m * 128:(m + 1) * 128]
                                    o_ap = msga[dt % 2][:, n * 128:(n + 1) * 128]
                                    if m == 0:
                                        ti = vector.tensor_scalar(
                                            out=o_ap, in0=h_ap, scalar1=e_ap,
                                            scalar2=None, op0=OP.mult)
                                    else:
                                        ti = vector.scalar_tensor_tensor(
                                            out=o_ap, in0=h_ap, scalar=e_ap,
                                            in1=o_ap, op0=OP.mult, op1=OP.add)
                                sV.inc(ti)
                                if m < 3:
                                    vector.wait_ge(sV.h, sV.n)
                            vector.wait_ge(sV.h, sV.n)
                            for n in range(4):
                                ti = vector.tensor_tensor(
                                    out=msga[dt % 2][:, n * 128:(n + 1) * 128],
                                    in0=msga[dt % 2][:, n * 128:(n + 1) * 128],
                                    in1=h1T[:, dt, n * 128:(n + 1) * 128],
                                    op=OP.add)
                                sV.inc(ti)

                        @block.tensor
                        def _(tensor, dt=dt):
                            tensor.wait_ge(sV.h, sV.n)
                            if dt < 2:
                                tensor.wait_ge(sSC.h, sSC.n)
                            for n in range(4):
                                ti = tensor.transpose(
                                    psT[dt % 2][:, n * 128:(n + 1) * 128],
                                    msga[dt % 2][:, n * 128:(n + 1) * 128],
                                    ident[:])
                            sT.inc(ti)

                        @block.vector
                        def _(vector, dt=dt, H=H):
                            vector.wait_ge(sT.h, sT.n)
                            for n in range(4):
                                tt = 2 * n + H
                                ti = vector.tensor_tensor(
                                    out=xr(tt)[:, dt * 128:(dt + 1) * 128],
                                    in0=xr(tt)[:, dt * 128:(dt + 1) * 128],
                                    in1=psT[dt % 2][:, n * 128:(n + 1) * 128],
                                    op=OP.add)
                                sV.inc(ti)

                emit_msg(1)
                emit_build_hT(half_tts[0], nmean1, rstd1, h1T, htmp,
                              lambda tt: tt // 2, xr)
                emit_msg(0)

                msg_all_done = sV.n
                emit_ln_stats(nmean2, rstd2,
                              wait_first=lambda e: e.wait_ge(sV.h, msg_all_done))

                ln2_v = sV.n
                ln2_s = sSC.n

                @block.gpsimd
                def _(gpsimd):
                    gpsimd.wait_ge(sV.h, ln2_v)
                    gpsimd.wait_ge(sSC.h, ln2_s)
                    gpsimd.dma_start(out=park.ap(),
                                     in_=xrB[:].rearrange("p a d -> p (a d)")
                                     ).then_inc(sGP.h, 16)
                    sGP.n += 16
                parked = sGP.n

        with ExitStack() as pff:
            h2T = pff.enter_context(nc.sbuf_tensor("h2T", [128, NDT, TPASS], BF16))
            htm2 = pff.enter_context(nc.sbuf_tensor("htm2", [128, D], BF16))
            act = pff.enter_context(nc.sbuf_tensor("act", [128, JLS, TPASS], BF16))
            wg = [pff.enter_context(nc.sbuf_tensor(f"wg{i}", [128, NDT, 128], BF16))
                  for i in range(2)]
            wu = [pff.enter_context(nc.sbuf_tensor(f"wu{i}", [128, NDT, 128], BF16))
                  for i in range(2)]
            wd = [pff.enter_context(nc.sbuf_tensor(f"wd{i}", [128, JLS, 512], BF16))
                  for i in range(2)]
            gsl = [pff.enter_context(nc.sbuf_tensor(f"gsl{i}", [128, TPASS], BF16))
                   for i in range(2)]

            gmm = {0: 0, 1: 0}
            umm = {0: 0, 1: 0}
            silu_ev = {0: 0, 1: 0}
            act_ev = {0: 0, 1: 0}
            dmm = {0: 0, 1: 0}
            dev = {0: 0, 1: 0}
            wdn = {0: 0, 1: 0}
            act_free = [0]

            for ffp in range(2):
                pass_tts = list(range(NTTP)) if ffp == 0 else list(range(NTTP, NTT))

                if ffp == 1:
                    @block.sync
                    def _(sync):
                        sync.wait_ge(sV.h, sV.n)
                        sync.dma_start(
                            out=out[0:512, :].rearrange("(a p) d -> p a d", p=128),
                            in_=xrA[:]).then_inc(sST.h, 16)
                        sST.n += 16
                        sync.wait_ge(sST.h, sST.n)
                        sync.wait_ge(sGP.h, parked)
                        sync.dma_start(out=xrA[:].rearrange("p a d -> p (a d)"),
                                       in_=park.ap()).then_inc(sIN.h, 16)
                        sIN.n += 16
                    xb_back = sIN.n

                    def xr_p(tt):
                        return xrA[:, tt - NTTP, :]

                    def first_wait(e):
                        e.wait_ge(sIN.h, xb_back)
                else:
                    def xr_p(tt):
                        return xrA[:, tt, :]

                    gin_total = sIN2.n

                    def first_wait(e):
                        e.wait_ge(sGP.h, parked)
                        e.wait_ge(sCTG.h, NST * 16)
                        e.wait_ge(sCTU.h, NST * 16)
                        e.wait_ge(sIN2.h, gin_total)

                emit_build_hT(pass_tts, nmean2, rstd2, h2T, htm2,
                              lambda tt: tt % NTTP, xr_p, extra_first=first_wait)
                h2T_sc = sSC.n

                for s in range(NST):
                    for jl in range(JLS):
                        @block.sync
                        def _(sync, s=s, jl=jl, ffp=ffp):
                            if ffp == 0 and s == 0 and jl == 0:
                                sync.wait_ge(sGP.h, parked)
                                sync.wait_ge(sCTG.h, NST * 16)
                                sync.wait_ge(sCTU.h, NST * 16)
                            if jl == 0:
                                sync.wait_ge(sAG.h, s + 1)
                            sync.wait_ge(sT.h, gmm[jl % 2])
                            sync.dma_start(
                                out=wg[jl % 2][:],
                                in_=agout[s].ap()[jl, 0].rearrange(
                                    "p (a b) -> p a b", b=128)
                                ).then_inc(sWG[jl % 2].h, 16)
                            sWG[jl % 2].n += 16
                            sync.wait_ge(sT.h, umm[jl % 2])
                            sync.dma_start(
                                out=wu[jl % 2][:],
                                in_=agout[s].ap()[jl, 1].rearrange(
                                    "p (a b) -> p a b", b=128)
                                ).then_inc(sWU[jl % 2].h, 16)
                            sWU[jl % 2].n += 16
                        wgl = sWG[jl % 2].n
                        wul = sWU[jl % 2].n

                        @block.tensor
                        def _(tensor, jl=jl):
                            tensor.wait_ge(sWG[jl % 2].h, wgl)
                            tensor.wait_ge(sSC.h, h2T_sc)
                            tensor.wait_ge(sSC.h, silu_ev[jl % 2])
                            for dt in range(NDT):
                                ti = tensor.matmul(
                                    psG[jl % 2][:], wg[jl % 2][:, dt, :],
                                    h2T[:, dt, :], start=(dt == 0),
                                    stop=(dt == NDT - 1))
                            sT.inc(ti)
                            gmm[jl % 2] = sT.n
                            tensor.wait_ge(sWU[jl % 2].h, wul)
                            tensor.wait_ge(sV.h, act_ev[jl % 2])
                            for dt in range(NDT):
                                ti = tensor.matmul(
                                    psU[jl % 2][:], wu[jl % 2][:, dt, :],
                                    h2T[:, dt, :], start=(dt == 0),
                                    stop=(dt == NDT - 1))
                            sT.inc(ti)
                            umm[jl % 2] = sT.n

                        @block.scalar
                        def _(scalar, jl=jl):
                            scalar.wait_ge(sT.h, gmm[jl % 2])
                            scalar.wait_ge(sV.h, act_ev[jl % 2])
                            ti = scalar.activation(out=gsl[jl % 2][:],
                                                   in_=psG[jl % 2][:], func=AF.Silu)
                            silu_ev[jl % 2] = sSC.inc(ti)

                        @block.vector
                        def _(vector, jl=jl):
                            vector.wait_ge(sT.h, umm[jl % 2])
                            vector.wait_ge(sSC.h, silu_ev[jl % 2])
                            if jl == 0:
                                vector.wait_ge(sT.h, act_free[0])
                            ti = vector.scalar_tensor_tensor(
                                out=act[:, jl, :], in0=psU[jl % 2][:], scalar=1.0,
                                in1=gsl[jl % 2][:], op0=OP.mult, op1=OP.mult)
                            act_ev[jl % 2] = sV.inc(ti)
                    act_ready = sV.n

                    for q in range(8):
                        @block.scalar
                        def _(scalar, s=s, q=q):
                            if q == 0:
                                scalar.wait_ge(sAG.h, s + 1)
                            scalar.wait_ge(sT.h, wdn[q % 2])
                            scalar.dma_start(
                                out=wd[q % 2][:],
                                in_=agout[s].ap()[:, 2, :, q * 512:(q + 1) * 512]
                                .rearrange("r p d -> p r d")
                                ).then_inc(sWD2[q % 2].h, 16)
                            sWD2[q % 2].n += 16
                        wdl = sWD2[q % 2].n

                        for tt_i in range(NTTP):
                            g = q * NTTP + tt_i

                            @block.tensor
                            def _(tensor, q=q, tt_i=tt_i, g=g):
                                tensor.wait_ge(sWD2[q % 2].h, wdl)
                                if g == 0:
                                    tensor.wait_ge(sV.h, act_ready)
                                tensor.wait_ge(sV.h, dev[g % 2])
                                for jl in range(JLS):
                                    ti = tensor.matmul(
                                        psD[g % 2][:],
                                        act[:, jl, tt_i * 128:(tt_i + 1) * 128],
                                        wd[q % 2][:, jl, :],
                                        start=(jl == 0), stop=(jl == JLS - 1))
                                sT.inc(ti)
                                dmm[g % 2] = sT.n
                                if tt_i == NTTP - 1:
                                    wdn[q % 2] = sT.n

                            @block.vector
                            def _(vector, q=q, tt_i=tt_i, g=g):
                                vector.wait_ge(sT.h, dmm[g % 2])
                                tt = pass_tts[tt_i]
                                ti = vector.tensor_tensor(
                                    out=xr_p(tt)[:, q * 512:(q + 1) * 512],
                                    in0=xr_p(tt)[:, q * 512:(q + 1) * 512],
                                    in1=psD[g % 2][:], op=OP.add)
                                dev[g % 2] = sV.inc(ti)
                    act_free[0] = sT.n

                if ffp == 1:
                    @block.sync
                    def _(sync):
                        sync.wait_ge(sV.h, sV.n)
                        sync.dma_start(
                            out=out[512:1024, :].rearrange("(a p) d -> p a d", p=128),
                            in_=xrA[:]).then_inc(sST.h, 16)
                        sST.n += 16
                        sync.wait_ge(sST.h, sST.n)

    return nc


def kernel(L0_attn_x, L0_ffn_gate_w, L0_ffn_gate_z, L0_ffn_gate_s,
           L0_ffn_up_w, L0_ffn_up_z, L0_ffn_up_s,
           L0_ffn_down_w, L0_ffn_down_z, L0_ffn_down_s):
    x = np.asarray(L0_attn_x, dtype=np.float32)
    gw = np.asarray(L0_ffn_gate_w, dtype=np.int32)
    gz = np.asarray(L0_ffn_gate_z, dtype=np.int32)
    gs = np.asarray(L0_ffn_gate_s, dtype=np.float32)
    uw = np.asarray(L0_ffn_up_w, dtype=np.int32)
    uz = np.asarray(L0_ffn_up_z, dtype=np.int32)
    us = np.asarray(L0_ffn_up_s, dtype=np.float32)
    dw = np.asarray(L0_ffn_down_w, dtype=np.int32)
    dz = np.asarray(L0_ffn_down_z, dtype=np.int32)
    ds = np.asarray(L0_ffn_down_s, dtype=np.float32)

    gwT = np.ascontiguousarray(gw.T)
    uwT = np.ascontiguousarray(uw.T)
    dwT = np.ascontiguousarray(dw.T)
    gsT = np.ascontiguousarray(gs.T)
    gzT = np.ascontiguousarray(gz.T)
    usT = np.ascontiguousarray(us.T)
    uzT = np.ascontiguousarray(uz.T)
    dsT = np.ascontiguousarray(ds.T)
    dzT = np.ascontiguousarray(dz.T)
    ident = np.eye(128, dtype=np.float32).astype(BF)

    nc = build_nc()
    in_maps = []
    for c in range(NC):
        jsel = np.concatenate(
            [np.arange(s * 1024 + c * 128, s * 1024 + (c + 1) * 128)
             for s in range(NST)])
        gidx = np.array([s * 8 + c for s in range(NST)])
        in_maps.append(dict(
            x_in=np.ascontiguousarray(x[:, c * SLOC:(c + 1) * SLOC, :]
                                      .reshape(T, D)),
            gT=np.ascontiguousarray(gwT[jsel].reshape(NST, 128, D)),
            uT=np.ascontiguousarray(uwT[jsel].reshape(NST, 128, D)),
            dTt=np.ascontiguousarray(dwT[:, jsel].reshape(NDT, 128, NST, 128)),
            gs_h=np.ascontiguousarray(gsT[jsel].reshape(NST, 128, 32)
                                      .transpose(1, 0, 2)),
            gz_h=np.ascontiguousarray(gzT[jsel].reshape(NST, 128, 32)
                                      .transpose(1, 0, 2)),
            us_h=np.ascontiguousarray(usT[jsel].reshape(NST, 128, 32)
                                      .transpose(1, 0, 2)),
            uz_h=np.ascontiguousarray(uzT[jsel].reshape(NST, 128, 32)
                                      .transpose(1, 0, 2)),
            ds_h=np.ascontiguousarray(dsT[:, gidx].reshape(NDT, 128, NST)
                                      .transpose(1, 0, 2)),
            dz_h=np.ascontiguousarray(dzT[:, gidx].reshape(NDT, 128, NST)
                                      .transpose(1, 0, 2)),
            idb=ident,
        ))

    res = run_bass_kernel_spmd(nc, in_maps, core_ids=list(range(NC)))

    outp = np.empty((B, S, D), dtype=np.float32)
    for c in range(NC):
        outp[:, c * SLOC:(c + 1) * SLOC, :] = \
            res.results[c]["out"].reshape(B, SLOC, D)
    return outp


# version 06 (final)
# speedup vs baseline: 1.1093x; 1.0038x over previous
import math
from contextlib import ExitStack

import numpy as np
import ml_dtypes

import concourse.bass as bass
import concourse.mybir as mybir
from concourse.bass_utils import run_bass_kernel_spmd

F32 = mybir.dt.float32
BF16 = mybir.dt.bfloat16
I32 = mybir.dt.int32
AF = mybir.ActivationFunctionType
OP = mybir.AluOpType
AX = mybir.AxisListType
BF = ml_dtypes.bfloat16

D = 4096
II = 14336
B = 4
S = 2048
NC = 8
SLOC = S // NC
T = B * SLOC
NTT = T // 128
NDT = D // 128
NST = 14
JLS = 8
TPASS = 512
NTTP = TPASS // 128
EPS = 1e-5
ATT_SCALE = 1.0 / math.sqrt(float(S))
SHARED_AG = True


class Ctr:
    def __init__(self, stack, nc, name):
        self.h = stack.enter_context(nc.semaphore(name))
        self.n = 0

    def inc(self, instr, k=1):
        instr.then_inc(self.h, k)
        self.n += k
        return self.n


def build_nc():
    nc = bass.Bass("TRN2", target_bir_lowering=False, debug=False)

    x_in = nc.dram_tensor("x_in", [T, D], F32, kind="ExternalInput").ap()
    gT = nc.dram_tensor("gT", [NST, 128, D], I32, kind="ExternalInput").ap()
    uT = nc.dram_tensor("uT", [NST, 128, D], I32, kind="ExternalInput").ap()
    dTt = nc.dram_tensor("dTt", [NDT, 128, NST, 128], I32, kind="ExternalInput").ap()
    gs_h = nc.dram_tensor("gs_h", [128, NST, NDT], F32, kind="ExternalInput").ap()
    gz_h = nc.dram_tensor("gz_h", [128, NST, NDT], I32, kind="ExternalInput").ap()
    us_h = nc.dram_tensor("us_h", [128, NST, NDT], F32, kind="ExternalInput").ap()
    uz_h = nc.dram_tensor("uz_h", [128, NST, NDT], I32, kind="ExternalInput").ap()
    ds_h = nc.dram_tensor("ds_h", [128, NDT, NST], F32, kind="ExternalInput").ap()
    dz_h = nc.dram_tensor("dz_h", [128, NDT, NST], I32, kind="ExternalInput").ap()
    idb = nc.dram_tensor("idb", [128, 128], BF16, kind="ExternalInput").ap()

    out = nc.dram_tensor("out", [T, D], F32, kind="ExternalOutput").ap()

    contrib = [nc.dram_tensor(f"ctb{s}", [1, 3, 128, D], BF16) for s in range(NST)]
    agout = [nc.dram_tensor(f"ago{s}", [NC, 3, 128, D], BF16,
                            addr_space="Shared" if SHARED_AG else "Local")
             for s in range(NST)]
    ar_in = nc.dram_tensor("ar_in", [128, 512], F32)
    ar_out = nc.dram_tensor("ar_out", [128, 512], F32)
    park = nc.dram_tensor("park", [128, NTTP * D], F32)

    with ExitStack() as ctx:
        ent = ctx.enter_context

        psG = [ent(nc.psum_tensor(f"psG{i}", [128, 512], F32)) for i in range(2)]
        psU = [ent(nc.psum_tensor(f"psU{i}", [128, 512], F32)) for i in range(2)]
        psD = [ent(nc.psum_tensor(f"psD{i}", [128, 512], F32)) for i in range(2)]
        psT = [ent(nc.psum_tensor(f"psT{i}", [128, 1024], BF16)) for i in range(2)]

        sIN = Ctr(ctx, nc, "sIN")
        sIN2 = Ctr(ctx, nc, "sIN2")
        sGP = Ctr(ctx, nc, "sGP")
        sCTD = Ctr(ctx, nc, "sCTD")
        sCTG = Ctr(ctx, nc, "sCTG")
        sCTU = Ctr(ctx, nc, "sCTU")
        sAG = Ctr(ctx, nc, "sAG")
        sAR = Ctr(ctx, nc, "sAR")
        sT = Ctr(ctx, nc, "sT")
        sV = Ctr(ctx, nc, "sV")
        sSC = Ctr(ctx, nc, "sSC")
        sDN = [Ctr(ctx, nc, f"sDN{i}") for i in range(2)]
        sGI = [Ctr(ctx, nc, f"sGI{i}") for i in range(2)]
        sWG = [Ctr(ctx, nc, f"sWG{i}") for i in range(2)]
        sWU = [Ctr(ctx, nc, f"sWU{i}") for i in range(2)]
        sWD2 = [Ctr(ctx, nc, f"sWD{i}") for i in range(2)]
        sST = Ctr(ctx, nc, "sST")

        ident = ent(nc.sbuf_tensor("ident", [128, 128], BF16))
        nmean1 = ent(nc.sbuf_tensor("nmean1", [128, NTT], F32))
        rstd1 = ent(nc.sbuf_tensor("rstd1", [128, NTT], F32))
        nmean2 = ent(nc.sbuf_tensor("nmean2", [128, NTT], F32))
        rstd2 = ent(nc.sbuf_tensor("rstd2", [128, NTT], F32))
        ssum = ent(nc.sbuf_tensor("ssum", [128, NTT], F32))
        ssq = ent(nc.sbuf_tensor("ssq", [128, NTT], F32))
        svar = ent(nc.sbuf_tensor("svar", [128, NTT], F32))
        ssd = ent(nc.sbuf_tensor("ssd", [128, NTT], F32))
        S0 = ent(nc.sbuf_tensor("S0", [128, NDT, 16], F32))
        S1 = ent(nc.sbuf_tensor("S1", [128, NDT, 16], F32))
        Ssc = ent(nc.sbuf_tensor("Ssc", [128, NDT, 16], F32))
        NMX = ent(nc.sbuf_tensor("NMX", [128, NDT * 4], F32))
        SSM = ent(nc.sbuf_tensor("SSM", [128, NDT * 4], F32))
        R4 = ent(nc.sbuf_tensor("R4", [128, NDT * 4], F32))
        GRK = 8
        grt = ent(nc.sbuf_tensor("grt", [128, GRK * 128], F32))
        gr_incs = []
        gsf = ent(nc.sbuf_tensor("gsf", [128, NST, NDT], F32))
        gzf = ent(nc.sbuf_tensor("gzf", [128, NST, NDT], F32))
        usf = ent(nc.sbuf_tensor("usf", [128, NST, NDT], F32))
        uzf = ent(nc.sbuf_tensor("uzf", [128, NST, NDT], F32))

        block = ent(nc.Block())

        with ExitStack() as p0:
            dsf = p0.enter_context(nc.sbuf_tensor("dsf", [128, NDT, NST], F32))
            dzf = p0.enter_context(nc.sbuf_tensor("dzf", [128, NDT, NST], F32))
            din = [p0.enter_context(nc.sbuf_tensor(f"din{i}", [128, NST * 128], I32))
                   for i in range(2)]
            ddq = [p0.enter_context(nc.sbuf_tensor(f"ddq{i}", [128, NST * 128], BF16))
                   for i in range(2)]
            stgd = p0.enter_context(nc.sbuf_tensor("stgd", [128, NST, D], BF16))

            @block.sync
            def _(sync):
                for dst, src in [(ident, idb), (gsf[:], gs_h), (usf[:], us_h),
                                 (dsf[:], ds_h)]:
                    sync.dma_start(out=dst if isinstance(dst, bass.AP) else dst[:],
                                   in_=src).then_inc(sIN.h, 16)
                    sIN.n += 16
            scales_in = sIN.n

            @block.gpsimd
            def _(gpsimd):
                for dst, src in [(gzf, gz_h), (uzf, uz_h), (dzf, dz_h)]:
                    gpsimd.dma_start(out=dst[:], in_=src).then_inc(sGP.h, 16)
                    sGP.n += 16
            scales_gp = sGP.n

            tr_cnt0 = sT.n
            ev_cnt0 = sSC.n
            dq_cnt0 = sV.n
            for dt in range(NDT):
                @block.sync
                def _(sync, dt=dt):
                    if dt >= 2:
                        sync.wait_ge(sV.h, dq_cnt0 + (dt - 1) * NST)
                    sync.dma_start(out=din[dt % 2][:], in_=dTt[dt]
                                   ).then_inc(sDN[dt % 2].h, 16)
                    sDN[dt % 2].n += 16

                @block.vector
                def _(vector, dt=dt):
                    if dt == 0:
                        vector.wait_ge(sGP.h, scales_gp)
                        vector.wait_ge(sIN.h, scales_in)
                    vector.wait_ge(sDN[dt % 2].h, (dt // 2 + 1) * 16)
                    if dt >= 2:
                        vector.wait_ge(sT.h, tr_cnt0 + ((dt - 1) * NST - 1) // 8 + 1)
                    for it in range(NST):
                        ti = vector.tensor_scalar(
                            out=ddq[dt % 2][:, it * 128:(it + 1) * 128],
                            in0=din[dt % 2][:, it * 128:(it + 1) * 128],
                            scalar1=dzf[:, dt, it:it + 1],
                            scalar2=dsf[:, dt, it:it + 1],
                            op0=OP.subtract, op1=OP.mult)
                        sV.inc(ti)

                @block.tensor
                def _(tensor, dt=dt):
                    for it in range(NST):
                        k = dt * NST + it
                        slot = k % 8
                        if slot == 0 and k >= 16:
                            tensor.wait_ge(sSC.h, ev_cnt0 + k // 8 - 1)
                        tensor.wait_ge(sV.h, dq_cnt0 + k + 1)
                        ti = tensor.transpose(
                            psT[(k // 8) % 2][:, slot * 128:(slot + 1) * 128],
                            ddq[dt % 2][:, it * 128:(it + 1) * 128], ident[:])
                        if slot == 7 or k == NDT * NST - 1:
                            sT.inc(ti)

            ngrp = NDT * NST // 8

            @block.scalar
            def _(scalar):
                for g in range(ngrp):
                    scalar.wait_ge(sT.h, tr_cnt0 + g + 1)
                    for j in range(8):
                        k = g * 8 + j
                        dt, it = divmod(k, NST)
                        ti = scalar.activation(
                            out=stgd[:, it, dt * 128:(dt + 1) * 128],
                            in_=psT[g % 2][:, j * 128:(j + 1) * 128], func=AF.Copy)
                    sSC.inc(ti)
            dn_ev_done = sSC.n

            @block.gpsimd
            def _(gpsimd):
                gpsimd.wait_ge(sSC.h, dn_ev_done)
                for s in range(NST):
                    gpsimd.dma_start(out=contrib[s].ap()[0, 2],
                                     in_=stgd[:, s, :]).then_inc(sCTD.h, 16)
                    sCTD.n += 16
            down_ct = sCTD.n

        xrA = ent(nc.sbuf_tensor("xrA", [128, NTTP, D], F32))

        with ExitStack() as pmid:
            xrB = pmid.enter_context(nc.sbuf_tensor("xrB", [128, NTTP, D], F32))

            def xr(tt):
                return xrA[:, tt, :] if tt < NTTP else xrB[:, tt - NTTP, :]

            @block.sync
            def _(sync):
                sync.wait_ge(sCTD.h, down_ct)
                sync.dma_start(out=xrA[:],
                               in_=x_in[0:512, :].rearrange("(a p) d -> p a d", p=128)
                               ).then_inc(sIN.h, 16)
                sIN.n += 16
                sync.dma_start(out=xrB[:],
                               in_=x_in[512:1024, :].rearrange("(a p) d -> p a d", p=128)
                               ).then_inc(sIN.h, 16)
                sIN.n += 16
            x_loaded = sIN.n

            with ExitStack() as p0b:
                gin = [p0b.enter_context(nc.sbuf_tensor(f"gin{i}", [128, D], I32))
                       for i in range(2)]
                gdq = p0b.enter_context(nc.sbuf_tensor("gdq", [128, D], BF16))
                stgg = p0b.enter_context(nc.sbuf_tensor("stgg", [128, NDT, 128], BF16))
                stgu = p0b.enter_context(nc.sbuf_tensor("stgu", [128, NDT, 128], BF16))

                gu_in0 = sIN2.n
                gu_dq0 = sV.n
                gu_tr0 = sT.n
                gu_ev0 = sSC.n
                for s in range(NST):
                    for half in range(2):
                        k = 2 * s + half
                        win, wdq = gin[k % 2], gdq
                        sf, zf = (gsf, gzf) if half == 0 else (usf, uzf)
                        stg = stgg if half == 0 else stgu

                        def emit_gin_load(eng, k, win):
                            if k == 0:
                                eng.wait_ge(sCTD.h, down_ct)
                            if k >= 2:
                                eng.wait_ge(sV.h, gu_dq0 + (k - 1) * NDT)
                            eng.dma_start(out=win[:], in_=gT[k // 2] if k % 2 == 0
                                          else uT[k // 2]
                                          ).then_inc(sGI[k % 2].h, 16)
                            sGI[k % 2].n += 16

                        if k % 2 == 0:
                            @block.scalar
                            def _(scalar, k=k, win=win):
                                emit_gin_load(scalar, k, win)
                        else:
                            @block.sync
                            def _(sync, k=k, win=win):
                                emit_gin_load(sync, k, win)

                        @block.vector
                        def _(vector, k=k, s=s, win=win, wdq=wdq, sf=sf, zf=zf):
                            vector.wait_ge(sGI[k % 2].h, (k // 2 + 1) * 16)
                            if k >= 1:
                                vector.wait_ge(sT.h, gu_tr0 + k * 4)
                            for dt in range(NDT):
                                ti = vector.tensor_scalar(
                                    out=wdq[:, dt * 128:(dt + 1) * 128],
                                    in0=win[:, dt * 128:(dt + 1) * 128],
                                    scalar1=zf[:, s, dt:dt + 1],
                                    scalar2=sf[:, s, dt:dt + 1],
                                    op0=OP.subtract, op1=OP.mult)
                                sV.inc(ti)

                        @block.tensor
                        def _(tensor, k=k, wdq=wdq):
                            for grp in range(4):
                                gk = k * 4 + grp
                                if gk >= 2:
                                    tensor.wait_ge(sSC.h, gu_ev0 + gk - 1)
                                tensor.wait_ge(sV.h, gu_dq0 + k * NDT + (grp + 1) * 8)
                                for j in range(8):
                                    dt = grp * 8 + j
                                    ti = tensor.transpose(
                                        psT[gk % 2][:, j * 128:(j + 1) * 128],
                                        wdq[:, dt * 128:(dt + 1) * 128], ident[:])
                                sT.inc(ti)

                        @block.scalar
                        def _(scalar, k=k, s=s, half=half, stg=stg):
                            sCX = sCTG if half == 0 else sCTU
                            if s >= 1:
                                scalar.wait_ge(sCX.h, s * 16)
                            for grp in range(4):
                                gk = k * 4 + grp
                                scalar.wait_ge(sT.h, gu_tr0 + gk + 1)
                                ti = scalar.activation(
                                    out=stg[:, grp * 8:(grp + 1) * 8, :],
                                    in_=psT[gk % 2][:].rearrange(
                                        "p (a b) -> p a b", a=8),
                                    func=AF.Copy)
                                sSC.inc(ti)
                            scalar.wait_ge(sSC.h, sSC.n)
                            scalar.dma_start(
                                out=contrib[s].ap()[0, half],
                                in_=stg[:].rearrange("p a b -> p (a b)")
                                ).then_inc(sCX.h, 16)
                            sCX.n += 16

                for s in range(2):
                    @block.gpsimd
                    def _(gpsimd, s=s):
                        gpsimd.wait_ge(sCTD.h, down_ct)
                        gpsimd.wait_ge(sCTG.h, (s + 1) * 16)
                        gpsimd.wait_ge(sCTU.h, (s + 1) * 16)
                        gpsimd.collective_compute(
                            "AllGather", OP.bypass,
                            replica_groups=[list(range(NC))],
                            ins=[contrib[s].ap()], outs=[agout[s].ap()],
                        ).then_inc(sAG.h, 1)
                        sAG.n += 1

            def emit_ln_stats(nmean, rstd, scratch, wait_first=None):
                @block.vector
                def _(vector):
                    vector.wait_ge(sIN.h, x_loaded)
                    if wait_first is not None:
                        wait_first(vector)
                    for tt in range(NTT):
                        ti = vector.reduce_sum(out=ssum[:, tt:tt + 1],
                                               in_=xr(tt), axis=AX.X)
                        sV.inc(ti)
                red_done = sV.n

                @block.scalar
                def _(scalar):
                    scalar.wait_ge(sIN.h, x_loaded)
                    if wait_first is not None:
                        wait_first(scalar)
                    sq0 = sSC.n
                    for tt in range(NTT):
                        if tt >= 1:
                            scalar.wait_ge(sSC.h, sq0 + tt)
                        ti = scalar.activation(out=scratch[:], in_=xr(tt),
                                               func=AF.Square,
                                               accum_out=ssq[:, tt:tt + 1])
                        sSC.inc(ti)
                sq_done = sSC.n

                @block.vector
                def _(vector):
                    vector.wait_ge(sV.h, red_done)
                    vector.wait_ge(sSC.h, sq_done)
                    ti = vector.tensor_scalar(out=nmean[:], in0=ssum[:],
                                              scalar1=-1.0 / D, scalar2=None,
                                              op0=OP.mult)
                    sV.inc(ti)
                    vector.wait_ge(sV.h, sV.n)
                    ti = vector.scalar_tensor_tensor(
                        out=svar[:], in0=nmean[:], scalar=1.0, in1=nmean[:],
                        op0=OP.mult, op1=OP.mult)
                    sV.inc(ti)
                    vector.wait_ge(sV.h, sV.n)
                    ti = vector.scalar_tensor_tensor(
                        out=svar[:], in0=ssq[:], scalar=1.0 / D, in1=svar[:],
                        op0=OP.mult, op1=OP.subtract)
                    sV.inc(ti)
                    vector.wait_ge(sV.h, sV.n)
                    ti = vector.tensor_scalar(out=svar[:], in0=svar[:],
                                              scalar1=EPS, scalar2=None,
                                              op0=OP.add)
                    sV.inc(ti)
                var_done = sV.n

                @block.scalar
                def _(scalar):
                    scalar.wait_ge(sV.h, var_done)
                    ti = scalar.activation(out=ssd[:], in_=svar[:], func=AF.Sqrt)
                    sSC.inc(ti)
                sd_done = sSC.n

                @block.vector
                def _(vector):
                    vector.wait_ge(sSC.h, sd_done)
                    ti = vector.reciprocal(out=rstd[:], in_=ssd[:])
                    sV.inc(ti)

            def emit_build_hT(tts, nmean, rstd, target, tmp, col_of_tt,
                              xget, extra_first=None):
                for i, tt in enumerate(tts):
                    @block.vector
                    def _(vector, tt=tt, i=i):
                        if i == 0 and extra_first is not None:
                            extra_first(vector)
                        if i == 0:
                            vector.wait_ge(sV.h, sV.n)
                        if i >= 1:
                            vector.wait_ge(sT.h, sT.n)
                        ti = vector.tensor_scalar(
                            out=tmp[:], in0=xget(tt),
                            scalar1=nmean[:, tt:tt + 1],
                            scalar2=rstd[:, tt:tt + 1],
                            op0=OP.add, op1=OP.mult)
                        sV.inc(ti)

                    @block.tensor
                    def _(tensor):
                        tensor.wait_ge(sV.h, sV.n)
                        ev0 = sSC.n
                        for grp in range(4):
                            tensor.wait_ge(sSC.h, ev0 + grp - 1)
                            for j in range(8):
                                dt = grp * 8 + j
                                ti = tensor.transpose(
                                    psT[grp % 2][:, j * 128:(j + 1) * 128],
                                    tmp[:, dt * 128:(dt + 1) * 128], ident[:])
                            sT.inc(ti)

                    @block.scalar
                    def _(scalar, tt=tt):
                        col = col_of_tt(tt)
                        base_t = sT.n - 4
                        for grp in range(4):
                            scalar.wait_ge(sT.h, base_t + grp + 1)
                            ti = scalar.activation(
                                out=target[:, grp * 8:(grp + 1) * 8,
                                            col * 128:(col + 1) * 128],
                                in_=psT[grp % 2][:].rearrange(
                                    "p (a b) -> p a b", a=8),
                                func=AF.Copy)
                            sSC.inc(ti)

            with ExitStack() as pat:
                h1T = pat.enter_context(
                    nc.sbuf_tensor("h1T", [128, NDT, 512], BF16))
                htmp = pat.enter_context(nc.sbuf_tensor("htmp", [128, D], BF16))
                msga = [pat.enter_context(
                    nc.sbuf_tensor(f"msga{i}", [128, 512], BF16))
                    for i in range(2)]

                @block.vector
                def _(vector):
                    ti = vector.memzero(S0[:].rearrange("p a b -> p (a b)"))
                    sV.inc(ti)
                    ti = vector.memzero(S1[:].rearrange("p a b -> p (a b)"))
                    sV.inc(ti)

                emit_ln_stats(nmean1, rstd1, htmp,
                              wait_first=lambda e: (
                                  e.wait_ge(sCTG.h, NST * 16),
                                  e.wait_ge(sCTU.h, NST * 16)))
                half_tts = {H: [2 * n + H for n in range(4)] for H in (0, 1)}

                def emit_grams(Sdst):
                    @block.vector
                    def _(vector, Sdst=Sdst):
                        vector.wait_ge(sSC.h, sSC.n)
                        vector.wait_ge(sV.h, sV.n)
                        for dt in range(NDT):
                            for n in range(4):
                                for m in range(n, 4):
                                    j = len(gr_incs)
                                    if j >= GRK:
                                        vector.wait_ge(sV.h, gr_incs[j - GRK])
                                    slot = j % GRK
                                    ti = vector.scalar_tensor_tensor(
                                        out=grt[:, slot * 128:(slot + 1) * 128],
                                        in0=h1T[:, dt, n * 128:(n + 1) * 128],
                                        scalar=1.0,
                                        in1=h1T[:, dt, m * 128:(m + 1) * 128],
                                        op0=OP.mult, op1=OP.mult,
                                        accum_out=Sdst[:, dt,
                                                       n * 4 + m:n * 4 + m + 1])
                                    gr_incs.append(sV.inc(ti))

                emit_build_hT(half_tts[0], nmean1, rstd1, h1T, htmp,
                              lambda tt: tt // 2, xr)
                emit_grams(S0)
                emit_build_hT(half_tts[1], nmean1, rstd1, h1T, htmp,
                              lambda tt: tt // 2, xr)
                emit_grams(S1)

                @block.vector
                def _(vector):
                    vector.wait_ge(sV.h, sV.n)
                    ti = vector.tensor_tensor(
                        out=Ssc[:].rearrange("p a b -> p (a b)"),
                        in0=S0[:].rearrange("p a b -> p (a b)"),
                        in1=S1[:].rearrange("p a b -> p (a b)"), op=OP.add)
                    sV.inc(ti)
                    vector.wait_ge(sV.h, sV.n)
                    for n in range(4):
                        for m in range(n):
                            ti = vector.tensor_copy(
                                Ssc[:, :, n * 4 + m:n * 4 + m + 1],
                                Ssc[:, :, m * 4 + n:m * 4 + n + 1])
                            sV.inc(ti)
                sym_done = sV.n

                @block.scalar
                def _(scalar):
                    scalar.wait_ge(sV.h, sym_done)
                    scalar.dma_start(out=ar_in.ap(),
                                     in_=Ssc[:].rearrange("p a b -> p (a b)")
                                     ).then_inc(sIN2.h, 16)
                    sIN2.n += 16
                ar_staged = sIN2.n

                @block.gpsimd
                def _(gpsimd):
                    gpsimd.wait_ge(sIN2.h, ar_staged)
                    gpsimd.collective_compute(
                        "AllReduce", OP.add,
                        replica_groups=[list(range(NC))],
                        ins=[ar_in.ap()], outs=[ar_out.ap()],
                    ).then_inc(sAR.h, 1)
                    sAR.n += 1
                    gpsimd.wait_ge(sAR.h, sAR.n)
                    gpsimd.dma_start(out=Ssc[:].rearrange("p a b -> p (a b)"),
                                     in_=ar_out.ap()).then_inc(sGP.h, 16)
                    sGP.n += 16
                    for s in range(2, NST):
                        gpsimd.wait_ge(sCTG.h, (s + 1) * 16)
                        gpsimd.wait_ge(sCTU.h, (s + 1) * 16)
                        gpsimd.collective_compute(
                            "AllGather", OP.bypass,
                            replica_groups=[list(range(NC))],
                            ins=[contrib[s].ap()], outs=[agout[s].ap()],
                        ).then_inc(sAG.h, 1)
                        sAG.n += 1
                ar_done = scores_back = sGP.n

                @block.vector
                def _(vector):
                    vector.wait_ge(sGP.h, scores_back)
                    ti = vector.tensor_reduce(
                        out=NMX[:],
                        in_=Ssc[:].rearrange("p a b -> p (a b)").rearrange(
                            "p (k m) -> p k m", m=4),
                        axis=AX.X, op=OP.max, negate=True)
                    sV.inc(ti)
                    vector.wait_ge(sV.h, sV.n)
                    for dt in range(NDT):
                        for n in range(4):
                            kk = dt * 4 + n
                            ti = vector.tensor_scalar(
                                out=Ssc[:, dt, n * 4:(n + 1) * 4],
                                in0=Ssc[:, dt, n * 4:(n + 1) * 4],
                                scalar1=NMX[:, kk:kk + 1], scalar2=None,
                                op0=OP.add)
                            sV.inc(ti)
                sub_done = sV.n

                @block.scalar
                def _(scalar):
                    scalar.wait_ge(sV.h, sub_done)
                    ti = scalar.activation(
                        out=Ssc[:].rearrange("p a b -> p (a b)"),
                        in_=Ssc[:].rearrange("p a b -> p (a b)"),
                        func=AF.Exp, scale=ATT_SCALE)
                    sSC.inc(ti)
                exp_done = sSC.n

                @block.vector
                def _(vector):
                    vector.wait_ge(sSC.h, exp_done)
                    ti = vector.reduce_sum(
                        out=SSM[:],
                        in_=Ssc[:].rearrange("p a b -> p (a b)").rearrange(
                            "p (k m) -> p k m", m=4), axis=AX.X)
                    sV.inc(ti)
                    vector.wait_ge(sV.h, sV.n)
                    ti = vector.reciprocal(out=R4[:], in_=SSM[:])
                    sV.inc(ti)
                    vector.wait_ge(sV.h, sV.n)
                    for dt in range(NDT):
                        for n in range(4):
                            kk = dt * 4 + n
                            ti = vector.tensor_scalar(
                                out=Ssc[:, dt, n * 4:(n + 1) * 4],
                                in0=Ssc[:, dt, n * 4:(n + 1) * 4],
                                scalar1=R4[:, kk:kk + 1], scalar2=None,
                                op0=OP.mult)
                            sV.inc(ti)
                prob_done = sV.n

                def emit_msg(H):
                    for dt in range(NDT):
                        @block.vector
                        def _(vector, dt=dt):
                            if dt == 0:
                                vector.wait_ge(sV.h, prob_done)
                                vector.wait_ge(sSC.h, sSC.n)
                            vector.wait_ge(sT.h, sT.n)
                            for m in range(4):
                                for n in range(4):
                                    e_ap = Ssc[:, dt, n * 4 + m:n * 4 + m + 1]
                                    h_ap = h1T[:, dt, m * 128:(m + 1) * 128]
                                    o_ap = msga[dt % 2][:, n * 128:(n + 1) * 128]
                                    if m == 0:
                                        ti = vector.tensor_scalar(
                                            out=o_ap, in0=h_ap, scalar1=e_ap,
                                            scalar2=None, op0=OP.mult)
                                    else:
                                        ti = vector.scalar_tensor_tensor(
                                            out=o_ap, in0=h_ap, scalar=e_ap,
                                            in1=o_ap, op0=OP.mult, op1=OP.add)
                                sV.inc(ti)
                                if m < 3:
                                    vector.wait_ge(sV.h, sV.n)
                            vector.wait_ge(sV.h, sV.n)
                            for n in range(4):
                                ti = vector.tensor_tensor(
                                    out=msga[dt % 2][:, n * 128:(n + 1) * 128],
                                    in0=msga[dt % 2][:, n * 128:(n + 1) * 128],
                                    in1=h1T[:, dt, n * 128:(n + 1) * 128],
                                    op=OP.add)
                                sV.inc(ti)

                        @block.tensor
                        def _(tensor, dt=dt):
                            tensor.wait_ge(sV.h, sV.n)
                            if dt < 2:
                                tensor.wait_ge(sSC.h, sSC.n)
                            for n in range(4):
                                ti = tensor.transpose(
                                    psT[dt % 2][:, n * 128:(n + 1) * 128],
                                    msga[dt % 2][:, n * 128:(n + 1) * 128],
                                    ident[:])
                            sT.inc(ti)

                        @block.vector
                        def _(vector, dt=dt, H=H):
                            vector.wait_ge(sT.h, sT.n)
                            for n in range(4):
                                tt = 2 * n + H
                                ti = vector.tensor_tensor(
                                    out=xr(tt)[:, dt * 128:(dt + 1) * 128],
                                    in0=xr(tt)[:, dt * 128:(dt + 1) * 128],
                                    in1=psT[dt % 2][:, n * 128:(n + 1) * 128],
                                    op=OP.add)
                                sV.inc(ti)

                emit_msg(1)
                emit_build_hT(half_tts[0], nmean1, rstd1, h1T, htmp,
                              lambda tt: tt // 2, xr)
                emit_msg(0)

                msg_all_done = sV.n
                emit_ln_stats(nmean2, rstd2, htmp,
                              wait_first=lambda e: e.wait_ge(sV.h, msg_all_done))

                ln2_v = sV.n
                ln2_s = sSC.n

                @block.gpsimd
                def _(gpsimd):
                    gpsimd.wait_ge(sV.h, ln2_v)
                    gpsimd.wait_ge(sSC.h, ln2_s)
                    gpsimd.dma_start(out=park.ap(),
                                     in_=xrB[:].rearrange("p a d -> p (a d)")
                                     ).then_inc(sGP.h, 16)
                    sGP.n += 16
                parked = sGP.n

        with ExitStack() as pff:
            h2T = pff.enter_context(nc.sbuf_tensor("h2T", [128, NDT, TPASS], BF16))
            htm2 = pff.enter_context(nc.sbuf_tensor("htm2", [128, D], BF16))
            act = pff.enter_context(nc.sbuf_tensor("act", [128, JLS, TPASS], BF16))
            wg = [pff.enter_context(nc.sbuf_tensor(f"wg{i}", [128, NDT, 128], BF16))
                  for i in range(2)]
            wu = [pff.enter_context(nc.sbuf_tensor(f"wu{i}", [128, NDT, 128], BF16))
                  for i in range(2)]
            wd = [pff.enter_context(nc.sbuf_tensor(f"wd{i}", [128, JLS, 512], BF16))
                  for i in range(2)]
            gsl = [pff.enter_context(nc.sbuf_tensor(f"gsl{i}", [128, TPASS], BF16))
                   for i in range(2)]

            gmm = {0: 0, 1: 0}
            umm = {0: 0, 1: 0}
            silu_ev = {0: 0, 1: 0}
            act_ev = {0: 0, 1: 0}
            dmm = {0: 0, 1: 0}
            dev = {0: 0, 1: 0}
            wdn = {0: 0, 1: 0}
            act_free = [0]

            for ffp in range(2):
                pass_tts = list(range(NTTP)) if ffp == 0 else list(range(NTTP, NTT))

                if ffp == 1:
                    @block.sync
                    def _(sync):
                        sync.wait_ge(sV.h, sV.n)
                        sync.dma_start(
                            out=out[0:512, :].rearrange("(a p) d -> p a d", p=128),
                            in_=xrA[:]).then_inc(sST.h, 16)
                        sST.n += 16
                        sync.wait_ge(sST.h, sST.n)
                        sync.wait_ge(sGP.h, parked)
                        sync.dma_start(out=xrA[:].rearrange("p a d -> p (a d)"),
                                       in_=park.ap()).then_inc(sIN.h, 16)
                        sIN.n += 16
                    xb_back = sIN.n

                    def xr_p(tt):
                        return xrA[:, tt - NTTP, :]

                    def first_wait(e):
                        e.wait_ge(sIN.h, xb_back)
                else:
                    def xr_p(tt):
                        return xrA[:, tt, :]

                    gin_total = sIN2.n

                    def first_wait(e):
                        e.wait_ge(sGP.h, parked)
                        e.wait_ge(sCTG.h, NST * 16)
                        e.wait_ge(sCTU.h, NST * 16)
                        e.wait_ge(sIN2.h, gin_total)

                emit_build_hT(pass_tts, nmean2, rstd2, h2T, htm2,
                              lambda tt: tt % NTTP, xr_p, extra_first=first_wait)
                h2T_sc = sSC.n

                for s in range(NST):
                    for jl in range(JLS):
                        @block.sync
                        def _(sync, s=s, jl=jl, ffp=ffp):
                            if ffp == 0 and s == 0 and jl == 0:
                                sync.wait_ge(sGP.h, parked)
                                sync.wait_ge(sCTG.h, NST * 16)
                                sync.wait_ge(sCTU.h, NST * 16)
                            if jl == 0:
                                sync.wait_ge(sAG.h, s + 1)
                            sync.wait_ge(sT.h, gmm[jl % 2])
                            sync.dma_start(
                                out=wg[jl % 2][:],
                                in_=agout[s].ap()[jl, 0].rearrange(
                                    "p (a b) -> p a b", b=128)
                                ).then_inc(sWG[jl % 2].h, 16)
                            sWG[jl % 2].n += 16
                            sync.wait_ge(sT.h, umm[jl % 2])
                            sync.dma_start(
                                out=wu[jl % 2][:],
                                in_=agout[s].ap()[jl, 1].rearrange(
                                    "p (a b) -> p a b", b=128)
                                ).then_inc(sWU[jl % 2].h, 16)
                            sWU[jl % 2].n += 16
                        wgl = sWG[jl % 2].n
                        wul = sWU[jl % 2].n

                        @block.tensor
                        def _(tensor, jl=jl):
                            tensor.wait_ge(sWG[jl % 2].h, wgl)
                            tensor.wait_ge(sSC.h, h2T_sc)
                            tensor.wait_ge(sSC.h, silu_ev[jl % 2])
                            for dt in range(NDT):
                                ti = tensor.matmul(
                                    psG[jl % 2][:], wg[jl % 2][:, dt, :],
                                    h2T[:, dt, :], start=(dt == 0),
                                    stop=(dt == NDT - 1))
                            sT.inc(ti)
                            gmm[jl % 2] = sT.n
                            tensor.wait_ge(sWU[jl % 2].h, wul)
                            tensor.wait_ge(sV.h, act_ev[jl % 2])
                            for dt in range(NDT):
                                ti = tensor.matmul(
                                    psU[jl % 2][:], wu[jl % 2][:, dt, :],
                                    h2T[:, dt, :], start=(dt == 0),
                                    stop=(dt == NDT - 1))
                            sT.inc(ti)
                            umm[jl % 2] = sT.n

                        @block.scalar
                        def _(scalar, jl=jl):
                            scalar.wait_ge(sT.h, gmm[jl % 2])
                            scalar.wait_ge(sV.h, act_ev[jl % 2])
                            ti = scalar.activation(out=gsl[jl % 2][:],
                                                   in_=psG[jl % 2][:], func=AF.Silu)
                            silu_ev[jl % 2] = sSC.inc(ti)

                        @block.vector
                        def _(vector, jl=jl):
                            vector.wait_ge(sT.h, umm[jl % 2])
                            vector.wait_ge(sSC.h, silu_ev[jl % 2])
                            if jl == 0:
                                vector.wait_ge(sT.h, act_free[0])
                            ti = vector.scalar_tensor_tensor(
                                out=act[:, jl, :], in0=psU[jl % 2][:], scalar=1.0,
                                in1=gsl[jl % 2][:], op0=OP.mult, op1=OP.mult)
                            act_ev[jl % 2] = sV.inc(ti)
                    act_ready = sV.n

                    for q in range(8):
                        @block.scalar
                        def _(scalar, s=s, q=q):
                            if q == 0:
                                scalar.wait_ge(sAG.h, s + 1)
                            scalar.wait_ge(sT.h, wdn[q % 2])
                            scalar.dma_start(
                                out=wd[q % 2][:],
                                in_=agout[s].ap()[:, 2, :, q * 512:(q + 1) * 512]
                                .rearrange("r p d -> p r d")
                                ).then_inc(sWD2[q % 2].h, 16)
                            sWD2[q % 2].n += 16
                        wdl = sWD2[q % 2].n

                        for tt_i in range(NTTP):
                            g = q * NTTP + tt_i

                            @block.tensor
                            def _(tensor, q=q, tt_i=tt_i, g=g):
                                tensor.wait_ge(sWD2[q % 2].h, wdl)
                                if g == 0:
                                    tensor.wait_ge(sV.h, act_ready)
                                tensor.wait_ge(sV.h, dev[g % 2])
                                for jl in range(JLS):
                                    ti = tensor.matmul(
                                        psD[g % 2][:],
                                        act[:, jl, tt_i * 128:(tt_i + 1) * 128],
                                        wd[q % 2][:, jl, :],
                                        start=(jl == 0), stop=(jl == JLS - 1))
                                sT.inc(ti)
                                dmm[g % 2] = sT.n
                                if tt_i == NTTP - 1:
                                    wdn[q % 2] = sT.n

                            @block.vector
                            def _(vector, q=q, tt_i=tt_i, g=g):
                                vector.wait_ge(sT.h, dmm[g % 2])
                                tt = pass_tts[tt_i]
                                ti = vector.tensor_tensor(
                                    out=xr_p(tt)[:, q * 512:(q + 1) * 512],
                                    in0=xr_p(tt)[:, q * 512:(q + 1) * 512],
                                    in1=psD[g % 2][:], op=OP.add)
                                dev[g % 2] = sV.inc(ti)
                    act_free[0] = sT.n

                if ffp == 1:
                    @block.sync
                    def _(sync):
                        sync.wait_ge(sV.h, sV.n)
                        sync.dma_start(
                            out=out[512:1024, :].rearrange("(a p) d -> p a d", p=128),
                            in_=xrA[:]).then_inc(sST.h, 16)
                        sST.n += 16
                        sync.wait_ge(sST.h, sST.n)

    return nc


def kernel(L0_attn_x, L0_ffn_gate_w, L0_ffn_gate_z, L0_ffn_gate_s,
           L0_ffn_up_w, L0_ffn_up_z, L0_ffn_up_s,
           L0_ffn_down_w, L0_ffn_down_z, L0_ffn_down_s):
    x = np.asarray(L0_attn_x, dtype=np.float32)
    gw = np.asarray(L0_ffn_gate_w, dtype=np.int32)
    gz = np.asarray(L0_ffn_gate_z, dtype=np.int32)
    gs = np.asarray(L0_ffn_gate_s, dtype=np.float32)
    uw = np.asarray(L0_ffn_up_w, dtype=np.int32)
    uz = np.asarray(L0_ffn_up_z, dtype=np.int32)
    us = np.asarray(L0_ffn_up_s, dtype=np.float32)
    dw = np.asarray(L0_ffn_down_w, dtype=np.int32)
    dz = np.asarray(L0_ffn_down_z, dtype=np.int32)
    ds = np.asarray(L0_ffn_down_s, dtype=np.float32)

    gwT = np.ascontiguousarray(gw.T)
    uwT = np.ascontiguousarray(uw.T)
    dwT = np.ascontiguousarray(dw.T)
    gsT = np.ascontiguousarray(gs.T)
    gzT = np.ascontiguousarray(gz.T)
    usT = np.ascontiguousarray(us.T)
    uzT = np.ascontiguousarray(uz.T)
    dsT = np.ascontiguousarray(ds.T)
    dzT = np.ascontiguousarray(dz.T)
    ident = np.eye(128, dtype=np.float32).astype(BF)

    nc = build_nc()
    in_maps = []
    for c in range(NC):
        jsel = np.concatenate(
            [np.arange(s * 1024 + c * 128, s * 1024 + (c + 1) * 128)
             for s in range(NST)])
        gidx = np.array([s * 8 + c for s in range(NST)])
        in_maps.append(dict(
            x_in=np.ascontiguousarray(x[:, c * SLOC:(c + 1) * SLOC, :]
                                      .reshape(T, D)),
            gT=np.ascontiguousarray(gwT[jsel].reshape(NST, 128, D)),
            uT=np.ascontiguousarray(uwT[jsel].reshape(NST, 128, D)),
            dTt=np.ascontiguousarray(dwT[:, jsel].reshape(NDT, 128, NST, 128)),
            gs_h=np.ascontiguousarray(gsT[jsel].reshape(NST, 128, 32)
                                      .transpose(1, 0, 2)),
            gz_h=np.ascontiguousarray(gzT[jsel].reshape(NST, 128, 32)
                                      .transpose(1, 0, 2)),
            us_h=np.ascontiguousarray(usT[jsel].reshape(NST, 128, 32)
                                      .transpose(1, 0, 2)),
            uz_h=np.ascontiguousarray(uzT[jsel].reshape(NST, 128, 32)
                                      .transpose(1, 0, 2)),
            ds_h=np.ascontiguousarray(dsT[:, gidx].reshape(NDT, 128, NST)
                                      .transpose(1, 0, 2)),
            dz_h=np.ascontiguousarray(dzT[:, gidx].reshape(NDT, 128, NST)
                                      .transpose(1, 0, 2)),
            idb=ident,
        ))

    res = run_bass_kernel_spmd(nc, in_maps, core_ids=list(range(NC)))

    outp = np.empty((B, S, D), dtype=np.float32)
    for c in range(NC):
        outp[:, c * SLOC:(c + 1) * SLOC, :] = \
            res.results[c]["out"].reshape(B, SLOC, D)
    return outp
